# version 08
# baseline (speedup 1.0000x reference)
import contextlib
import numpy as np

import concourse.bass as bass
import concourse.mybir as mybir
from concourse.bass_utils import run_bass_kernel_spmd

F32 = mybir.dt.float32
BF16 = mybir.dt.bfloat16
I32 = mybir.dt.int32
I16 = mybir.dt.int16
U8 = mybir.dt.uint8
AF = mybir.ActivationFunctionType
ALU = mybir.AluOpType

P = 128
D = 2048
NTOK = 4096
H = 16
DK = 128
M = 4096
MLP = 16384
TA = 512
NT_A = NTOK // TA
TB = 512
NT_B = NTOK // TB
EC = D // P
SQD = float(DK) ** 0.5
GELU_A = 0.044715
GELU_S = 2.0 * 0.7978845608028654

ENGINES = ("sync", "scalar", "vector", "tensor", "gpsimd")


class Buf:
    __slots__ = ("w", "r")

    def __init__(self, init=None):
        self.w = dict(init) if init else {}
        self.r = dict(init) if init else {}


def _merge(dst, src):
    for kk, v in src.items():
        if dst.get(kk, -1) < v:
            dst[kk] = v


class K:

    def __init__(self, nc):
        self.nc = nc
        self.ops = {e: [] for e in ENGINES}
        self.cnt = {e: 0 for e in ENGINES}
        self.qcnt = {}

    def _deps(self, reads, writes, extra):
        deps = {}
        for b in reads:
            _merge(deps, b.w)
        for b in writes:
            _merge(deps, b.w)
            _merge(deps, b.r)
        for tok in extra:
            if tok is not None:
                _merge(deps, {tok[0]: tok[1]})
        return deps

    def _record(self, tok, reads, writes):
        d = {tok[0]: tok[1]}
        for b in reads:
            _merge(b.r, d)
        for b in writes:
            _merge(b.w, d)

    def op(self, eng, fn, reads=(), writes=(), extra=()):
        deps = self._deps(reads, writes, extra)
        self.cnt[eng] += 1
        tok = (("e", eng), self.cnt[eng])
        self.ops[eng].append((fn, deps, ("e", eng), 1))
        self._record(tok, reads, writes)
        return tok

    def dma(self, eng, q, fn, reads=(), writes=(), extra=()):
        deps = self._deps(reads, writes, extra)
        self.qcnt[q] = self.qcnt.get(q, 0) + 16
        tok = (("q", q), self.qcnt[q])
        self.ops[eng].append((fn, deps, ("q", q), 16))
        self._record(tok, reads, writes)
        return tok

    def barrier_toks(self):
        toks = {("e", e): self.cnt[e] for e in ENGINES if self.cnt[e] > 0}
        toks.update({("q", q): v for q, v in self.qcnt.items()})
        return toks

    def emit(self, es):
        sems = {}
        for e in ENGINES:
            if self.cnt[e] > 0:
                sems[("e", e)] = es.enter_context(self.nc.semaphore(f"se_{e}"))
        for q in self.qcnt:
            sems[("q", q)] = es.enter_context(self.nc.semaphore(f"sq_{q}"))
        block = es.enter_context(self.nc.Block())
        final_cnt = dict(self.cnt)
        for e in ENGINES:
            if not self.ops[e]:
                continue

            def make(e):
                oplist = self.ops[e]

                def body(eng):
                    waited = {}
                    for fn, deps, skey, inc in oplist:
                        for key in sorted(deps, key=repr):
                            val = deps[key]
                            if key == ("e", e):
                                if e == "tensor":
                                    continue
                                if val >= final_cnt[e]:
                                    continue
                            if waited.get(key, 0) >= val:
                                continue
                            eng.wait_ge(sems[key], val)
                            waited[key] = val
                        fn(eng).then_inc(sems[skey], inc)
                return body

            getattr(block, e)(make(e))


def build_nc(debug=False):
    nc = bass.Bass("TRN2", target_bir_lowering=False, debug=False)

    x_h = nc.dram_tensor("x", [NTOK, D], F32, kind="ExternalInput").ap()
    wq_h = nc.dram_tensor("wq", [D, D], F32, kind="ExternalInput").ap()
    wk_h = nc.dram_tensor("wk", [M, D], F32, kind="ExternalInput").ap()
    wv_h = nc.dram_tensor("wv", [M, D], F32, kind="ExternalInput").ap()
    wg_h = nc.dram_tensor("wg", [D, D], F32, kind="ExternalInput").ap()
    wo_h = nc.dram_tensor("wo", [D, D], F32, kind="ExternalInput").ap()
    gp_h = nc.dram_tensor("gp", [MLP, D // 2], I32, kind="ExternalInput").ap()
    gs_h = nc.dram_tensor("gs", [MLP, D // 32], F32, kind="ExternalInput").ap()
    up_h = nc.dram_tensor("up", [MLP, D // 2], I32, kind="ExternalInput").ap()
    us_h = nc.dram_tensor("us", [MLP, D // 32], F32, kind="ExternalInput").ap()
    dp_h = nc.dram_tensor("dp", [D, MLP // 2], I32, kind="ExternalInput").ap()
    ds_h = nc.dram_tensor("ds", [D, MLP // 32], F32, kind="ExternalInput").ap()
    out_h = nc.dram_tensor("out", [NTOK, D], F32, kind="ExternalOutput").ap()

    ikind = "ExternalOutput" if debug else "Internal"
    wq16 = nc.dram_tensor("wq16", [D, D], BF16, kind="Internal").ap()
    wg16 = nc.dram_tensor("wg16", [D, D], BF16, kind="Internal").ap()
    wo16 = nc.dram_tensor("wo16", [D, D], BF16, kind="Internal").ap()
    k16 = nc.dram_tensor("k16", [M, D], BF16, kind="Internal").ap()
    wqT_t = nc.dram_tensor("wqT_t", [EC, P, EC, P], BF16, kind="Internal").ap()
    wgT_t = nc.dram_tensor("wgT_t", [EC, P, EC, P], BF16, kind="Internal").ap()
    woT_t = nc.dram_tensor("woT_t", [4, 2, P, 8, 512], BF16, kind="Internal").ap()
    kT = nc.dram_tensor("kT", [D, M], BF16, kind="Internal").ap()
    natg = nc.dram_tensor("natg", [MLP, D], BF16, kind="Internal").ap()
    natu = nc.dram_tensor("natu", [MLP, D], BF16, kind="Internal").ap()
    natd = nc.dram_tensor("natd", [D, MLP], BF16, kind="Internal").ap()
    wgT_f = nc.dram_tensor("wgT_f", [MLP // P, P, EC, P], BF16, kind="Internal").ap()
    wuT_f = nc.dram_tensor("wuT_f", [MLP // P, P, EC, P], BF16, kind="Internal").ap()
    wdT = nc.dram_tensor("wdT", [MLP, D], BF16, kind="Internal").ap()
    xls = nc.dram_tensor("xls", [NTOK, D], BF16, kind=ikind).ap()
    yls = nc.dram_tensor("yls", [NTOK, D], BF16, kind="Internal").ap()
    ys = nc.dram_tensor("ys", [NTOK, D], F32, kind=ikind).ap()

    k = K(nc)
    es = contextlib.ExitStack()
    sbuf = nc.sbuf_tensor
    psum = nc.psum_tensor

    stt = [es.enter_context(sbuf(f"stt{i}", [P, 32], F32)) for i in range(2)]
    b_stt = [Buf() for _ in range(2)]
    stg16 = [es.enter_context(sbuf("stg16_0", [P, 4096], BF16)),
             es.enter_context(sbuf("stg16_1", [P, 2048], BF16))]
    b_stg16 = [Buf() for _ in range(2)]
    xstg = [es.enter_context(sbuf(f"xstg{i}", [P, D], F32)) for i in range(2)]
    b_xstg = [Buf() for _ in range(2)]
    lstg = [es.enter_context(sbuf(f"lstg{i}", [P, D], BF16)) for i in range(2)]
    b_lstg = [Buf() for _ in range(2)]
    xln = es.enter_context(sbuf("xlnf", [P, EC * TA], BF16))
    qf = es.enter_context(sbuf("qf", [P, H * TA], BF16))
    gf = es.enter_context(sbuf("gf", [P, H * TA], BF16))
    ogf = es.enter_context(sbuf("ogf", [P, H * TA], BF16))
    b_xln = [Buf() for _ in range(EC)]
    b_qf, b_gf, b_ogf = Buf(), Buf(), Buf()
    wsl = [es.enter_context(sbuf(f"wsl{i}", [P, EC * P], BF16)) for i in range(2)]
    b_wsl = [Buf() for _ in range(2)]
    wosl = [es.enter_context(sbuf(f"wosl{i}", [P, 8 * 512], BF16)) for i in range(2)]
    b_wosl = [Buf() for _ in range(2)]
    ktsl = [es.enter_context(sbuf(f"ktsl{i}", [P, M], BF16)) for i in range(2)]
    b_ktsl = [Buf() for _ in range(2)]
    vsl = [es.enter_context(sbuf(f"vsl{i}", [P, 32 * DK], BF16)) for i in range(2)]
    b_vsl = [[Buf() for _ in range(4)] for _ in range(2)]
    ensb = [es.enter_context(sbuf(f"ensb{i}", [P, TA], BF16)) for i in range(6)]
    b_ensb = [Buf() for _ in range(6)]
    ypc = [es.enter_context(sbuf(f"ypc{i}", [P, 512], F32)) for i in range(4)]
    b_ypc = [Buf() for _ in range(4)]
    xpc = [es.enter_context(sbuf(f"xpc{i}", [P, 512], F32)) for i in range(2)]
    b_xpc = [Buf() for _ in range(2)]
    dvd_sh = es.enter_context(sbuf("dvd", [P, 2048], BF16))
    b_dvd_sh = Buf()
    dec = []
    for i in range(2):
        dec.append(dict(
            u8=es.enter_context(sbuf(f"du8_{i}", [P, 1024], U8)),
            m=es.enter_context(sbuf(f"dm_{i}", [P, 2048], I16)),
            s=es.enter_context(sbuf(f"dsx_{i}", [P, 2048], I16)),
            c=es.enter_context(sbuf(f"dcx_{i}", [P, 2048], I16)),
            vd=dvd_sh, sc=es.enter_context(sbuf(f"dsc_{i}", [P, 64], F32)),
            sc16=es.enter_context(sbuf(f"dsc16_{i}", [P, 64], BF16)),
            b_u8=Buf(), b_m=Buf(), b_s=Buf(), b_c=Buf(), b_vd=b_dvd_sh,
            b_sc=Buf(), b_sc16=Buf(),
        ))
    ogt = [es.enter_context(sbuf(f"ogt{i}", [P, TA], BF16)) for i in range(2)]
    b_ogt = [Buf() for _ in range(2)]

    p_en = [es.enter_context(psum(f"p_en{i}", [P, 512], F32)) for i in range(2)]
    b_pen = [Buf() for _ in range(2)]
    p_out = [es.enter_context(psum(f"p_out{i}", [P, 512], F32)) for i in range(2)]
    b_pout = [Buf() for _ in range(2)]
    p_mm = [es.enter_context(psum(f"p_mm{i}", [P, 512], F32)) for i in range(4)]
    b_pmm = [Buf() for _ in range(4)]

    b_wq16, b_wg16, b_wo16, b_k16 = Buf(), Buf(), Buf(), Buf()
    b_wqT, b_wgT, b_woT, b_kT = Buf(), Buf(), Buf(), Buf()
    b_natg = [Buf() for _ in range(MLP // P)]
    b_natu = [Buf() for _ in range(MLP // P)]
    b_natd = [Buf() for _ in range(128)]
    b_wgTf, b_wuTf, b_wdT = Buf(), Buf(), Buf()
    b_xls = [[Buf() for _ in range(4)] for _ in range(NT_A)]
    b_yls = [[Buf() for _ in range(4)] for _ in range(NT_A)]
    b_ys = [[Buf() for _ in range(4)] for _ in range(32)]

    def cast_to_16(src, dst, nrow, b_dst, tag):
        for c0 in range(nrow // P):
            sl = c0 % 2
            k.dma("gpsimd", f"stg{sl}",
                  lambda g, sl=sl, c0=c0, src=src:
                  g.dma_start(out=stg16[sl][:, 0:D],
                              in_=src[c0 * P:(c0 + 1) * P, :]),
                  writes=(b_stg16[sl],))
            k.dma("sync", f"stgw{sl}",
                  lambda s, sl=sl, c0=c0, dst=dst:
                  s.dma_start(out=dst[c0 * P:(c0 + 1) * P, :],
                              in_=stg16[sl][:, 0:D]),
                  reads=(b_stg16[sl],), writes=(b_dst,))

    cast_to_16(wq_h, wq16, D, b_wq16, "wq")
    cast_to_16(wg_h, wg16, D, b_wg16, "wg")
    cast_to_16(wo_h, wo16, D, b_wo16, "wo")
    cast_to_16(wk_h, k16, M, b_k16, "wk")

    def xbar_tiled(src16, b_src, dstT, b_dst):
        for ec in range(EC):
            sl = ec % 2
            k.dma("sync", f"stg{sl}",
                  lambda s, ec=ec, sl=sl, src16=src16:
                  s.dma_start_transpose(stg16[sl][:, 0:D], src16[:, ec * P:(ec + 1) * P]),
                  reads=(b_src,), writes=(b_stg16[sl],))
            for half in range(2):
                k.dma("sync", f"stgw{sl}",
                      lambda s, ec=ec, sl=sl, half=half, dstT=dstT:
                      s.dma_start(out=dstT[half * 8:(half + 1) * 8, :, ec].transpose([1, 0, 2]),
                                  in_=stg16[sl][:, half * 1024:(half + 1) * 1024]
                                  .rearrange("p (fc col) -> p fc col", col=P)),
                      reads=(b_stg16[sl],), writes=(b_dst,))

    xbar_tiled(wq16, b_wq16, wqT_t, b_wqT)
    xbar_tiled(wg16, b_wg16, wgT_t, b_wgT)

    for vc in range(EC):
        sl = vc % 2
        k.dma("sync", f"stg{sl}",
              lambda s, vc=vc, sl=sl:
              s.dma_start_transpose(stg16[sl][:, 0:D], wo16[:, vc * P:(vc + 1) * P]),
              reads=(b_wo16,), writes=(b_stg16[sl],))
        for eq in range(4):
            k.dma("sync", f"stgw{sl}",
                  lambda s, vc=vc, sl=sl, eq=eq:
                  s.dma_start(out=woT_t[eq, vc // 8, :, vc % 8],
                              in_=stg16[sl][:, eq * 512:(eq + 1) * 512]),
                  reads=(b_stg16[sl],), writes=(b_woT,))

    for c in range(EC):
        sl = 0
        k.dma("sync", f"stg{sl}",
              lambda s, c=c, sl=sl:
              s.dma_start_transpose(stg16[sl][:, 0:M], k16[:, c * P:(c + 1) * P]),
              reads=(b_k16,), writes=(b_stg16[sl],))
        k.dma("sync", f"stgw{sl}",
              lambda s, c=c, sl=sl:
              s.dma_start(out=kT.rearrange("(c p) m -> c p m", p=P)[c],
                          in_=stg16[sl][:, 0:M]),
              reads=(b_stg16[sl],), writes=(b_kT,))

    def decode_subtile(tail_eng, pack_ap, sc_ap, nat_out_ap, b_nat, slot):
        d = dec[slot]
        k.dma("gpsimd", f"du8{slot}",
              lambda g, pack_ap=pack_ap, d=d:
              g.dma_start(out=d["u8"][:, :], in_=pack_ap),
              writes=(d["b_u8"],))
        k.dma("gpsimd", f"dsc{slot}",
              lambda g, sc_ap=sc_ap, d=d:
              g.dma_start(out=d["sc"][:, :], in_=sc_ap),
              writes=(d["b_sc"],))
        w16 = d["u8"].ap().bitcast(I16)
        for j in range(4):
            if j == 0:
                k.op("vector", lambda e, d=d, w16=w16:
                     e.tensor_single_scalar(out=d["m"][:, 0:512], in_=w16, scalar=7,
                                            op=ALU.bitwise_and),
                     reads=(d["b_u8"],), writes=(d["b_m"],))
            else:
                k.op("vector", lambda e, d=d, w16=w16, j=j:
                     e.tensor_scalar(out=d["m"][:, j * 512:(j + 1) * 512], in0=w16,
                                     scalar1=4 * j, scalar2=7,
                                     op0=ALU.logical_shift_right, op1=ALU.bitwise_and),
                     reads=(d["b_u8"],), writes=(d["b_m"],))
        for j in range(4):
            k.op("vector", lambda e, d=d, w16=w16, j=j:
                 e.tensor_scalar(out=d["s"][:, j * 512:(j + 1) * 512], in0=w16,
                                 scalar1=8 << (4 * j), scalar2=12 - 4 * j,
                                 op0=ALU.bitwise_and, op1=ALU.logical_shift_left),
                 reads=(d["b_u8"],), writes=(d["b_s"],))
        k.op("vector", lambda e, d=d:
             e.tensor_single_scalar(out=d["c"][:, :], in_=d["m"][:, :], scalar=2, op=ALU.min),
             reads=(d["b_m"],), writes=(d["b_c"],))
        k.op("vector", lambda e, d=d:
             e.tensor_scalar(out=d["m"][:, :], in0=d["m"][:, :], scalar1=252, scalar2=254,
                             op0=ALU.add, op1=ALU.max),
             reads=(d["b_m"],), writes=(d["b_m"],))
        k.op("vector", lambda e, d=d:
             e.tensor_single_scalar(out=d["m"][:, :], in_=d["m"][:, :], scalar=6,
                                    op=ALU.logical_shift_left),
             reads=(d["b_m"],), writes=(d["b_m"],))
        k.op("vector", lambda e, d=d:
             e.tensor_tensor(out=d["m"][:, :], in0=d["m"][:, :], in1=d["s"][:, :],
                             op=ALU.add),
             reads=(d["b_m"], d["b_s"]), writes=(d["b_m"],))
        corrf = d["s"].ap().bitcast(BF16)
        k.op(tail_eng, lambda e, d=d, corrf=corrf:
             e.tensor_single_scalar(out=corrf, in_=d["c"][:, :], scalar=0.5, op=ALU.mult),
             reads=(d["b_c"],), writes=(d["b_s"],))
        val = d["c"].ap().bitcast(BF16)
        k.op(tail_eng, lambda e, d=d, corrf=corrf, val=val:
             e.tensor_mul(out=val, in0=d["m"].ap().bitcast(BF16), in1=corrf),
             reads=(d["b_m"], d["b_s"]), writes=(d["b_c"],))
        k.op(tail_eng, lambda e, d=d:
             e.tensor_copy(out=d["sc16"][:, :], in_=d["sc"][:, :]),
             reads=(d["b_sc"],), writes=(d["b_sc16"],))
        vdr = d["vd"].ap().rearrange("p (g four) -> p four g", four=4)
        for j in range(4):
            k.op(tail_eng, lambda e, d=d, val=val, vdr=vdr, j=j:
                 e.tensor_tensor(
                     out=vdr[:, j, :].rearrange("p (gb gg) -> p gb gg", gg=8),
                     in0=val[:, j * 512:(j + 1) * 512].rearrange("p (gb gg) -> p gb gg", gg=8),
                     in1=d["sc16"].ap().unsqueeze(2).to_broadcast([P, 64, 8]),
                     op=ALU.mult),
                 reads=(d["b_c"], d["b_sc16"]), writes=(d["b_vd"],))
        k.dma("gpsimd", f"dnw{slot}",
              lambda g, nat_out_ap=nat_out_ap, d=d:
              g.dma_start(out=nat_out_ap, in_=d["vd"][:, :]),
              reads=(d["b_vd"],), writes=(b_nat,))

    decode_jobs = []
    for j in range(MLP // P):
        r = j
        decode_jobs.append((gp_h[r * P:(r + 1) * P, :], gs_h[r * P:(r + 1) * P, :],
                            natg[r * P:(r + 1) * P, :], b_natg[r]))
        decode_jobs.append((up_h[r * P:(r + 1) * P, :], us_h[r * P:(r + 1) * P, :],
                            natu[r * P:(r + 1) * P, :], b_natu[r]))
        s8, r2 = j // 16, j % 16
        decode_jobs.append((dp_h[r2 * P:(r2 + 1) * P, s8 * 1024:(s8 + 1) * 1024],
                            ds_h[r2 * P:(r2 + 1) * P, s8 * 64:(s8 + 1) * 64],
                            natd[r2 * P:(r2 + 1) * P, s8 * 2048:(s8 + 1) * 2048],
                            b_natd[r2 * 8 + s8]))
    n_jobs = len(decode_jobs)
    djobs = [("gpsimd" if i % 3 != 0 else "vector", decode_jobs[i]) for i in range(n_jobs)]
    dec_state = {"i": 0}

    def emit_decode(n):
        stop = min(dec_state["i"] + n, n_jobs)
        while dec_state["i"] < stop:
            i = dec_state["i"]
            eng, (pa, sa, na, bn) = djobs[i]
            decode_subtile(eng, pa, sa, na, bn, i % 2)
            dec_state["i"] += 1

    def emit_ffn_transposes(idx):
        G = idx
        for nat, b_nat, dstT, b_dst in ((natg, b_natg, wgT_f, b_wgTf),
                                        (natu, b_natu, wuT_f, b_wuTf)):
            rdeps = tuple(b_nat[G * 16 + i] for i in range(16))
            for ec in range(EC):
                sl = ec % 2
                k.dma("sync", f"stg{sl}",
                      lambda s, nat=nat, G=G, ec=ec, sl=sl:
                      s.dma_start_transpose(
                          stg16[sl][:, 0:2048],
                          nat[G * 2048:(G + 1) * 2048, ec * P:(ec + 1) * P]),
                      reads=rdeps, writes=(b_stg16[sl],))
                for half in range(2):
                    k.dma("sync", f"stgw{sl}",
                          lambda s, dstT=dstT, G=G, ec=ec, sl=sl, half=half:
                          s.dma_start(
                              out=dstT[G * 16 + half * 8:G * 16 + (half + 1) * 8, :, ec]
                              .transpose([1, 0, 2]),
                              in_=stg16[sl][:, half * 1024:(half + 1) * 1024]
                              .rearrange("p (mc col) -> p mc col", col=P)),
                          reads=(b_stg16[sl],), writes=(b_dst,))
        for mc in range(idx * 16, (idx + 1) * 16):
            sl = mc % 2
            rdeps = tuple(b_natd[r * 8 + (mc // 16)] for r in range(16))
            k.dma("sync", f"stg{sl}",
                  lambda s, mc=mc, sl=sl:
                  s.dma_start_transpose(stg16[sl][:, 0:D], natd[:, mc * P:(mc + 1) * P]),
                  reads=rdeps, writes=(b_stg16[sl],))
            k.dma("sync", f"stgw{sl}",
                  lambda s, mc=mc, sl=sl:
                  s.dma_start(out=wdT.rearrange("(mc p) h -> mc p h", p=P)[mc],
                              in_=stg16[sl][:, 0:D]),
                  reads=(b_stg16[sl],), writes=(b_wdT,))

    def ln_chunk(xb, b_xb, outb, b_outb, si, save_slot=None):
        st = stt[si]
        b_st = b_stt[si]
        for g4 in range(4):
            k.op("vector", lambda v, g4=g4, xb=xb, st=st:
                 v.bn_stats(out=st[:, g4 * 6:(g4 + 1) * 6],
                            in_=xb[:, g4 * 512:(g4 + 1) * 512]),
                 reads=(b_xb,), writes=(b_st,))
        k.op("vector", lambda v, st=st: v.bn_aggr(out=st[:, 24:26], in_=st[:, 0:24]),
             reads=(b_st,), writes=(b_st,))
        k.op("vector", lambda v, st=st:
             v.tensor_scalar_add(out=st[:, 25:26], in0=st[:, 25:26], scalar1=1e-5),
             reads=(b_st,), writes=(b_st,))
        k.op("scalar", lambda s, st=st:
             s.activation(out=st[:, 31:32], in_=st[:, 25:26], func=AF.Sqrt),
             reads=(b_st,), writes=(b_st,))
        k.op("vector", lambda v, st=st: v.reciprocal(out=st[:, 26:27], in_=st[:, 31:32]),
             reads=(b_st,), writes=(b_st,))
        k.op("vector", lambda v, st=st:
             v.tensor_mul(out=st[:, 27:28], in0=st[:, 25:26], in1=st[:, 26:27]),
             reads=(b_st,), writes=(b_st,))
        k.op("vector", lambda v, st=st:
             v.tensor_mul(out=st[:, 28:29], in0=st[:, 27:28], in1=st[:, 26:27]),
             reads=(b_st,), writes=(b_st,))
        k.op("vector", lambda v, st=st:
             v.tensor_scalar(out=st[:, 28:29], in0=st[:, 28:29], scalar1=-0.5, scalar2=1.5,
                             op0=ALU.mult, op1=ALU.add),
             reads=(b_st,), writes=(b_st,))
        k.op("vector", lambda v, st=st:
             v.tensor_mul(out=st[:, 29:30], in0=st[:, 26:27], in1=st[:, 28:29]),
             reads=(b_st,), writes=(b_st,))
        k.op("vector", lambda v, st=st:
             v.scalar_tensor_tensor(out=st[:, 30:31], in0=st[:, 24:25], scalar=-1.0,
                                    in1=st[:, 29:30], op0=ALU.mult, op1=ALU.mult),
             reads=(b_st,), writes=(b_st,))
        k.op("scalar", lambda s, xb=xb, outb=outb, st=st:
             s.activation(out=outb, in_=xb, func=AF.Identity,
                          scale=st[:, 29:30], bias=st[:, 30:31]),
             reads=(b_xb, b_st), writes=(b_outb,))

    def phase_a_tile(t):
        t0 = t * TA
        for tc in range(4):
            sl = tc % 2
            k.dma("sync", f"xst{sl}",
                  lambda s, t0=t0, tc=tc, sl=sl:
                  s.dma_start(out=xstg[sl][:, :], in_=x_h[t0 + tc * P:t0 + (tc + 1) * P, :]),
                  writes=(b_xstg[sl],))
            ln_chunk(xstg[sl][:, :], b_xstg[sl], lstg[sl][:, :], b_lstg[sl], sl)
            k.dma("sync", f"lst{sl}",
                  lambda s, t0=t0, tc=tc, sl=sl:
                  s.dma_start(out=xls[t0 + tc * P:t0 + (tc + 1) * P, :], in_=lstg[sl][:, :]),
                  reads=(b_lstg[sl],), writes=(b_xls[t][tc],))
        xt_last = None
        for ec in range(EC):
            xt_last = k.dma("sync", "xbar",
                            lambda s, t0=t0, ec=ec:
                            s.dma_start_transpose(xln[:, ec * TA:(ec + 1) * TA],
                                                  xls[t0:t0 + TA, ec * P:(ec + 1) * P]),
                            reads=tuple(b_xls[t]), writes=(b_xln[ec],))
        for ec in range(EC):
            _merge(b_xln[ec].w, {xt_last[0]: xt_last[1]})
        for wT, b_wT, dstbuf, b_dst, act in ((wqT_t, b_wqT, qf, b_qf, None),
                                             (wgT_t, b_wgT, gf, b_gf, AF.Silu)):
            for fc in range(EC):
                sl = fc % 2
                k.dma("sync", f"wsl{sl}",
                      lambda sy, wT=wT, fc=fc, sl=sl:
                      sy.dma_start(out=wsl[sl][:, :].rearrange("p (ec col) -> p ec col", col=P),
                                  in_=wT[fc]),
                      reads=(b_wT,), writes=(b_wsl[sl],))
                pm = fc % 4
                for ecc in range(EC):
                    k.op("tensor", lambda te, sl=sl, ecc=ecc, pm=pm:
                         te.matmul(out=p_mm[pm][:, :],
                                   lhsT=wsl[sl][:, ecc * P:(ecc + 1) * P],
                                   rhs=xln[:, ecc * TA:ecc * TA + TA],
                                   start=(ecc == 0), stop=(ecc == EC - 1)),
                         reads=(b_wsl[sl], b_xln[ecc]), writes=(b_pmm[pm],))
                if act is None:
                    k.op("vector", lambda v, fc=fc, pm=pm, dstbuf=dstbuf:
                         v.tensor_copy(out=dstbuf[:, fc * TA:(fc + 1) * TA],
                                       in_=p_mm[pm][:, :]),
                         reads=(b_pmm[pm],), writes=(b_dst,))
                else:
                    k.op("scalar", lambda s, fc=fc, pm=pm, dstbuf=dstbuf, act=act:
                         s.activation(out=dstbuf[:, fc * TA:(fc + 1) * TA],
                                      in_=p_mm[pm][:, :], func=act),
                         reads=(b_pmm[pm],), writes=(b_dst,))
        for h in range(H):
            sl = h % 2
            k.dma("sync", f"kts{sl}",
                  lambda s, h=h, sl=sl:
                  s.dma_start(out=ktsl[sl][:, :],
                              in_=kT.rearrange("(c p) m -> c p m", p=P)[h]),
                  reads=(b_kT,), writes=(b_ktsl[sl],))
            vt = None
            for q4 in range(4):
                vt = k.dma("gpsimd", f"vsl{sl}",
                           lambda g, h=h, sl=sl, q4=q4:
                           g.dma_start(
                               out=vsl[sl][:, q4 * 1024:(q4 + 1) * 1024]
                               .rearrange("p (c dd) -> p c dd", dd=DK),
                               in_=wv_h.rearrange("(c p) e -> p c e", p=P)
                               [:, q4 * 8:(q4 + 1) * 8, h * DK:(h + 1) * DK]),
                           writes=(b_vsl[sl][q4],))
            for q4 in range(4):
                _merge(b_vsl[sl][q4].w, {vt[0]: vt[1]})
            po = h % 2
            en_pool = [p_en[0], p_en[1], p_mm[0], p_mm[1], p_mm[2], p_mm[3]]
            b_en_pool = [b_pen[0], b_pen[1], b_pmm[0], b_pmm[1], b_pmm[2], b_pmm[3]]
            for cc in range(36):
                if cc < 32:
                    c, pes = cc, cc % 6
                    k.op("tensor", lambda te, sl=sl, c=c, pes=pes, h=h:
                         te.matmul(out=en_pool[pes][:, :],
                                   lhsT=ktsl[sl][:, c * P:(c + 1) * P],
                                   rhs=qf[:, h * TA:(h + 1) * TA],
                                   start=True, stop=True),
                         reads=(b_ktsl[sl], b_qf), writes=(b_en_pool[pes],))
                    k.op("scalar", lambda s, pes=pes:
                         s.activation(out=ensb[pes][:, :], in_=en_pool[pes][:, :],
                                      func=AF.Silu, scale=SQD),
                         reads=(b_en_pool[pes],), writes=(b_ensb[pes],))
                if cc >= 4:
                    c, pes = cc - 4, (cc - 4) % 6
                    k.op("tensor", lambda te, sl=sl, c=c, pes=pes, po=po:
                         te.matmul(out=p_out[po][:, :],
                                   lhsT=vsl[sl][:, c * DK:(c + 1) * DK],
                                   rhs=ensb[pes][:, :],
                                   start=(c == 0), stop=(c == 31)),
                         reads=(b_vsl[sl][c // 8], b_ensb[pes]), writes=(b_pout[po],))
            k.op("scalar", lambda sc, h=h, po=po:
                 sc.activation(out=ogt[po][:, :], in_=p_out[po][:, :],
                               func=AF.Identity, scale=SQD),
                 reads=(b_pout[po],), writes=(b_ogt[po],))
            k.op("gpsimd", lambda g, h=h, po=po:
                 g.tensor_mul(out=ogf[:, h * TA:(h + 1) * TA],
                              in0=ogt[po][:, :], in1=gf[:, h * TA:(h + 1) * TA]),
                 reads=(b_ogt[po], b_gf), writes=(b_ogf,))
            emit_decode(3)
        for eq in range(4):
            for vh in range(2):
                sl = (eq * 2 + vh) % 2
                k.dma("sync", f"wos{sl}",
                      lambda sy, eq=eq, vh=vh, sl=sl:
                      sy.dma_start(out=wosl[sl][:, :]
                                  .rearrange("p (vc col) -> p vc col", col=512),
                                  in_=woT_t[eq, vh]),
                      reads=(b_woT,), writes=(b_wosl[sl],))
                for ts in range(4):
                    for vc8 in range(8):
                        vc = vh * 8 + vc8
                        k.op("tensor",
                             lambda te, ts=ts, vc=vc, vc8=vc8, sl=sl, vh=vh:
                             te.matmul(out=p_mm[ts][:, :],
                                       lhsT=ogf[:, vc * TA + ts * P:vc * TA + (ts + 1) * P],
                                       rhs=wosl[sl][:, vc8 * 512:(vc8 + 1) * 512],
                                       start=(vh == 0 and vc8 == 0),
                                       stop=(vh == 1 and vc8 == 7)),
                             reads=(b_ogf, b_wosl[sl]), writes=(b_pmm[ts],))
            for ts in range(4):
                xsl = ts % 2
                k.dma("sync", f"xpc{xsl}",
                      lambda s, t0=t0, eq=eq, ts=ts, xsl=xsl:
                      s.dma_start(out=xpc[xsl][:, :],
                                  in_=x_h[t0 + ts * P:t0 + (ts + 1) * P,
                                          eq * 512:(eq + 1) * 512]),
                      writes=(b_xpc[xsl],))
                k.op("vector", lambda v, ts=ts, xsl=xsl:
                     v.tensor_add(out=ypc[ts][:, :], in0=p_mm[ts][:, :], in1=xpc[xsl][:, :]),
                     reads=(b_pmm[ts], b_xpc[xsl]), writes=(b_ypc[ts],))
                k.dma("sync", f"ypc{ts}",
                      lambda s, t0=t0, eq=eq, ts=ts:
                      s.dma_start(out=ys[t0 + ts * P:t0 + (ts + 1) * P,
                                         eq * 512:(eq + 1) * 512],
                                  in_=ypc[ts][:, :]),
                      reads=(b_ypc[ts],), writes=(b_ys[t * 4 + ts][eq],))
        for tc in range(4):
            sl = tc % 2
            k.dma("sync", f"xst{sl}",
                  lambda s, t0=t0, tc=tc, sl=sl:
                  s.dma_start(out=xstg[sl][:, :], in_=ys[t0 + tc * P:t0 + (tc + 1) * P, :]),
                  reads=tuple(b_ys[t * 4 + tc]), writes=(b_xstg[sl],))
            ln_chunk(xstg[sl][:, :], b_xstg[sl], lstg[sl][:, :], b_lstg[sl], sl)
            k.dma("sync", f"lst{sl}",
                  lambda s, t0=t0, tc=tc, sl=sl:
                  s.dma_start(out=yls[t0 + tc * P:t0 + (tc + 1) * P, :], in_=lstg[sl][:, :]),
                  reads=(b_lstg[sl],), writes=(b_yls[t][tc],))

    for t in range(NT_A):
        phase_a_tile(t)
        if t >= 1:
            emit_ffn_transposes(t - 1)
    emit_decode(n_jobs)
    emit_ffn_transposes(7)

    bar = k.barrier_toks()

    def buf_after():
        return Buf(bar)

    yln = xln
    b_yln = [buf_after() for _ in range(EC)]
    inter = qf
    b_inter = buf_after()
    accA = gf.ap().bitcast(F32)
    accB = ogf.ap().bitcast(F32)
    acc_ap = [accA[:, 0:D], accA[:, D:2 * D], accB[:, 0:D], accB[:, D:2 * D]]
    b_acc = [buf_after() for _ in range(4)]
    ystg = [xstg[i].ap() for i in range(2)]
    b_ystg = [buf_after() for _ in range(2)]
    ostg = [stg16[0].ap().bitcast(F32), stg16[0].ap().bitcast(F32)]
    b_ostg0 = buf_after()
    b_ostg = [b_ostg0, b_ostg0]
    wgsl = [ktsl[i].ap() for i in range(2)]
    b_wgsl = [buf_after() for _ in range(2)]
    wusl = [vsl[i].ap() for i in range(2)]
    b_wusl = [buf_after() for _ in range(2)]
    wdsl = [wosl[0].ap(), wosl[1].ap()]
    b_wdsl = [buf_after() for _ in range(2)]
    glt = [ypc[0], ypc[1], ypc[2], ypc[3], xpc[0]]
    b_glt = [buf_after() for _ in range(5)]
    p_dn = [p_en[0].ap(), p_en[1].ap(), p_out[0].ap(), p_out[1].ap()]
    b_pdn = [buf_after() for _ in range(4)]
    b_pgu = [buf_after() for _ in range(4)]

    def phase_b_tile(b):
        t0 = b * TB
        yt_last = None
        for ec in range(EC):
            yt_last = k.dma("sync", "xbar",
                            lambda s, t0=t0, ec=ec:
                            s.dma_start_transpose(yln[:, ec * TB:(ec + 1) * TB],
                                                  yls[t0:t0 + TB, ec * P:(ec + 1) * P]),
                            reads=tuple(b_yls[b]), writes=(b_yln[ec],))
        for ec in range(EC):
            _merge(b_yln[ec].w, {yt_last[0]: yt_last[1]})
        for G in range(8):
            for mcl in range(16):
                mc = G * 16 + mcl
                sl = (mcl // 2) % 2
                half = (mcl % 2) * 2048
                pg = mcl % 2
                pu = 2 + mcl % 2
                if mcl % 2 == 0:
                    k.dma("sync", f"wgl{sl}",
                          lambda sy, mc=mc, sl=sl:
                          sy.dma_start(
                              out=wgsl[sl].rearrange("p (two n) -> p two n", n=2048),
                              in_=wgT_f[mc:mc + 2].rearrange("mc p ec col -> mc p (ec col)")
                              .transpose([1, 0, 2])),
                          reads=(b_wgTf,), writes=(b_wgsl[sl],))
                    k.dma("sync", f"wul{sl}",
                          lambda sy, mc=mc, sl=sl:
                          sy.dma_start(
                              out=wusl[sl].rearrange("p (two n) -> p two n", n=2048),
                              in_=wuT_f[mc:mc + 2].rearrange("mc p ec col -> mc p (ec col)")
                              .transpose([1, 0, 2])),
                          reads=(b_wuTf,), writes=(b_wusl[sl],))
                for ecc in range(EC):
                    k.op("tensor", lambda te, sl=sl, ecc=ecc, pg=pg, half=half:
                         te.matmul(out=p_mm[pg][:, :],
                                   lhsT=wgsl[sl][:, half + ecc * P:half + (ecc + 1) * P],
                                   rhs=yln[:, ecc * TB:(ecc + 1) * TB],
                                   start=(ecc == 0), stop=(ecc == EC - 1)),
                         reads=(b_wgsl[sl], b_yln[ecc]), writes=(b_pgu[pg],))
                for ecc in range(EC):
                    k.op("tensor", lambda te, sl=sl, ecc=ecc, pu=pu, half=half:
                         te.matmul(out=p_mm[pu][:, :],
                                   lhsT=wusl[sl][:, half + ecc * P:half + (ecc + 1) * P],
                                   rhs=yln[:, ecc * TB:(ecc + 1) * TB],
                                   start=(ecc == 0), stop=(ecc == EC - 1)),
                         reads=(b_wusl[sl], b_yln[ecc]), writes=(b_pgu[pu],))
                k.op("scalar", lambda s, pg=pg:
                     s.activation(out=glt[0][:, :], in_=p_mm[pg][:, :], func=AF.Square),
                     reads=(b_pgu[pg],), writes=(b_glt[0],))
                k.op("vector", lambda v:
                     v.tensor_scalar(out=glt[1][:, :], in0=glt[0][:, :], scalar1=GELU_A,
                                     scalar2=1.0, op0=ALU.mult, op1=ALU.add),
                     reads=(b_glt[0],), writes=(b_glt[1],))
                k.op("vector", lambda v, pg=pg:
                     v.tensor_mul(out=glt[2][:, :], in0=p_mm[pg][:, :], in1=glt[1][:, :]),
                     reads=(b_pgu[pg], b_glt[1]), writes=(b_glt[2],))
                k.op("scalar", lambda s:
                     s.activation(out=glt[3][:, :], in_=glt[2][:, :], func=AF.Sigmoid,
                                  scale=GELU_S),
                     reads=(b_glt[2],), writes=(b_glt[3],))
                k.op("vector", lambda v, pg=pg:
                     v.tensor_mul(out=glt[4][:, :], in0=p_mm[pg][:, :], in1=glt[3][:, :]),
                     reads=(b_pgu[pg], b_glt[3]), writes=(b_glt[4],))
                k.op("vector", lambda v, pu=pu, mcl=mcl:
                     v.tensor_mul(out=inter[:, mcl * TB:(mcl + 1) * TB],
                                  in0=glt[4][:, :], in1=p_mm[pu][:, :]),
                     reads=(b_glt[4], b_pgu[pu]), writes=(b_inter,))
            for hq in range(4):
                for half in range(2):
                    k.dma("sync", f"wdl{half}",
                          lambda s, G=G, hq=hq, half=half:
                          s.dma_start(
                              out=wdsl[half].rearrange("p (mc n) -> p mc n", n=512),
                              in_=wdT[(G * 16 + half * 8) * P:(G * 16 + (half + 1) * 8) * P,
                                      hq * 512:(hq + 1) * 512]
                              .rearrange("(mc p) n -> p mc n", p=P)),
                          reads=(b_wdT,), writes=(b_wdsl[half],))
                for mcl in range(16):
                    sl = mcl // 8
                    off = (mcl % 8) * 512
                    for ts in range(4):
                        k.op("tensor", lambda te, mcl=mcl, ts=ts, sl=sl, off=off:
                             te.matmul(out=p_dn[ts],
                                       lhsT=inter[:, mcl * TB + ts * P:mcl * TB + (ts + 1) * P],
                                       rhs=wdsl[sl][:, off:off + 512],
                                       start=(mcl == 0), stop=(mcl == 15)),
                             reads=(b_inter, b_wdsl[sl]), writes=(b_pdn[ts],))
                for ts in range(4):
                    if G == 0:
                        k.op("vector", lambda v, ts=ts, hq=hq:
                             v.tensor_copy(out=acc_ap[ts][:, hq * 512:(hq + 1) * 512],
                                           in_=p_dn[ts]),
                             reads=(b_pdn[ts],), writes=(b_acc[ts],))
                    else:
                        k.op("vector", lambda v, ts=ts, hq=hq:
                             v.tensor_add(out=acc_ap[ts][:, hq * 512:(hq + 1) * 512],
                                          in0=acc_ap[ts][:, hq * 512:(hq + 1) * 512],
                                          in1=p_dn[ts]),
                             reads=(b_pdn[ts], b_acc[ts]), writes=(b_acc[ts],))
        for ts in range(4):
            sl = ts % 2
            k.dma("sync", f"yst{sl}",
                  lambda s, t0=t0, ts=ts, sl=sl:
                  s.dma_start(out=ystg[sl], in_=ys[t0 + ts * P:t0 + (ts + 1) * P, :]),
                  reads=tuple(b_ys[b * 4 + ts]), writes=(b_ystg[sl],))
            k.op("vector", lambda v, ts=ts, sl=sl:
                 v.tensor_add(out=ostg[sl], in0=acc_ap[ts][:, :], in1=ystg[sl]),
                 reads=(b_acc[ts], b_ystg[sl]), writes=(b_ostg[sl],))
            k.dma("sync", f"ost{sl}",
                  lambda s, t0=t0, ts=ts, sl=sl:
                  s.dma_start(out=out_h[t0 + ts * P:t0 + (ts + 1) * P, :], in_=ostg[sl]),
                  reads=(b_ostg[sl],), writes=())

    for b in range(NT_B):
        phase_b_tile(b)

    fin = [(("q", "ost0"), k.qcnt.get("ost0", 0)), (("q", "ost1"), k.qcnt.get("ost1", 0))]
    k.op("sync", lambda s: s.nop(), extra=fin)
    k.op("gpsimd", lambda g: g.nop(), extra=fin)

    k.emit(es)
    return nc, es


def kernel(L0_attn_x, L0_attn_Wq, L0_attn_k_weight, L0_attn_v_weight, L0_attn_Wg,
           L0_attn_Wout, L0_ffn_gate_W_packed, L0_ffn_gate_scales, L0_ffn_up_W_packed,
           L0_ffn_up_scales, L0_ffn_down_W_packed, L0_ffn_down_scales):
    x = np.ascontiguousarray(np.asarray(L0_attn_x, dtype=np.float32))
    weights = dict(
        wq=np.ascontiguousarray(np.asarray(L0_attn_Wq, dtype=np.float32)),
        wk=np.ascontiguousarray(np.asarray(L0_attn_k_weight, dtype=np.float32)),
        wv=np.ascontiguousarray(np.asarray(L0_attn_v_weight, dtype=np.float32)),
        wg=np.ascontiguousarray(np.asarray(L0_attn_Wg, dtype=np.float32)),
        wo=np.ascontiguousarray(np.asarray(L0_attn_Wout, dtype=np.float32)),
        gp=np.ascontiguousarray(np.asarray(L0_ffn_gate_W_packed, dtype=np.int32)),
        gs=np.ascontiguousarray(np.asarray(L0_ffn_gate_scales, dtype=np.float32)),
        up=np.ascontiguousarray(np.asarray(L0_ffn_up_W_packed, dtype=np.int32)),
        us=np.ascontiguousarray(np.asarray(L0_ffn_up_scales, dtype=np.float32)),
        dp=np.ascontiguousarray(np.asarray(L0_ffn_down_W_packed, dtype=np.int32)),
        ds=np.ascontiguousarray(np.asarray(L0_ffn_down_scales, dtype=np.float32)),
    )
    nc, es = build_nc(debug=False)
    try:
        in_maps = [dict(weights, x=np.ascontiguousarray(x[b])) for b in range(8)]
        res = run_bass_kernel_spmd(nc, in_maps, core_ids=list(range(8)))
    finally:
        es.close()
    return np.stack([r["out"] for r in res.results], axis=0)


if __name__ == "__main__":
    rng = np.random.RandomState(0)
    print("smoke build only")
    nc, es = build_nc()
    es.close()
    print("built ok")


# version 09
# speedup vs baseline: 1.2679x; 1.2679x over previous
import contextlib
import numpy as np

import concourse.bass as bass
import concourse.mybir as mybir
from concourse.bass_utils import run_bass_kernel_spmd

F32 = mybir.dt.float32
BF16 = mybir.dt.bfloat16
I32 = mybir.dt.int32
I16 = mybir.dt.int16
U8 = mybir.dt.uint8
AF = mybir.ActivationFunctionType
ALU = mybir.AluOpType

P = 128
D = 2048
NTOK = 4096
H = 16
DK = 128
M = 4096
MLP = 16384
TA = 512
NT_A = NTOK // TA
TB = 512
NT_B = NTOK // TB
EC = D // P
SQD = float(DK) ** 0.5
GELU_A = 0.044715
GELU_S = 2.0 * 0.7978845608028654

ENGINES = ("sync", "scalar", "vector", "tensor", "gpsimd")


class Buf:
    __slots__ = ("w", "r")

    def __init__(self, init=None):
        self.w = dict(init) if init else {}
        self.r = dict(init) if init else {}


def _merge(dst, src):
    for kk, v in src.items():
        if dst.get(kk, -1) < v:
            dst[kk] = v


class K:

    def __init__(self, nc):
        self.nc = nc
        self.ops = {e: [] for e in ENGINES}
        self.cnt = {e: 0 for e in ENGINES}
        self.qcnt = {}

    def _deps(self, reads, writes, extra):
        deps = {}
        for b in reads:
            _merge(deps, b.w)
        for b in writes:
            _merge(deps, b.w)
            _merge(deps, b.r)
        for tok in extra:
            if tok is not None:
                _merge(deps, {tok[0]: tok[1]})
        return deps

    def _record(self, tok, reads, writes):
        d = {tok[0]: tok[1]}
        for b in reads:
            _merge(b.r, d)
        for b in writes:
            _merge(b.w, d)

    def op(self, eng, fn, reads=(), writes=(), extra=()):
        deps = self._deps(reads, writes, extra)
        self.cnt[eng] += 1
        tok = (("e", eng), self.cnt[eng])
        self.ops[eng].append((fn, deps, ("e", eng), 1))
        self._record(tok, reads, writes)
        return tok

    def dma(self, eng, q, fn, reads=(), writes=(), extra=()):
        deps = self._deps(reads, writes, extra)
        self.qcnt[q] = self.qcnt.get(q, 0) + 16
        tok = (("q", q), self.qcnt[q])
        self.ops[eng].append((fn, deps, ("q", q), 16))
        self._record(tok, reads, writes)
        return tok

    def barrier_toks(self):
        toks = {("e", e): self.cnt[e] for e in ENGINES if self.cnt[e] > 0}
        toks.update({("q", q): v for q, v in self.qcnt.items()})
        return toks

    def emit(self, es):
        sems = {}
        for e in ENGINES:
            if self.cnt[e] > 0:
                sems[("e", e)] = es.enter_context(self.nc.semaphore(f"se_{e}"))
        for q in self.qcnt:
            sems[("q", q)] = es.enter_context(self.nc.semaphore(f"sq_{q}"))
        block = es.enter_context(self.nc.Block())
        final_cnt = dict(self.cnt)
        for e in ENGINES:
            if not self.ops[e]:
                continue

            def make(e):
                oplist = self.ops[e]

                def body(eng):
                    waited = {}
                    for fn, deps, skey, inc in oplist:
                        for key in sorted(deps, key=repr):
                            val = deps[key]
                            if key == ("e", e):
                                if e == "tensor":
                                    continue
                                if val >= final_cnt[e]:
                                    continue
                            if waited.get(key, 0) >= val:
                                continue
                            eng.wait_ge(sems[key], val)
                            waited[key] = val
                        fn(eng).then_inc(sems[skey], inc)
                return body

            getattr(block, e)(make(e))


def build_nc(debug=False):
    nc = bass.Bass("TRN2", target_bir_lowering=False, debug=False)

    x_h = nc.dram_tensor("x", [NTOK, D], F32, kind="ExternalInput").ap()
    wq_h = nc.dram_tensor("wq", [D, D], F32, kind="ExternalInput").ap()
    wk_h = nc.dram_tensor("wk", [M, D], F32, kind="ExternalInput").ap()
    wv_h = nc.dram_tensor("wv", [M, D], F32, kind="ExternalInput").ap()
    wg_h = nc.dram_tensor("wg", [D, D], F32, kind="ExternalInput").ap()
    wo_h = nc.dram_tensor("wo", [D, D], F32, kind="ExternalInput").ap()
    gp_h = nc.dram_tensor("gp", [MLP, D // 2], I32, kind="ExternalInput").ap()
    gs_h = nc.dram_tensor("gs", [MLP, D // 32], F32, kind="ExternalInput").ap()
    up_h = nc.dram_tensor("up", [MLP, D // 2], I32, kind="ExternalInput").ap()
    us_h = nc.dram_tensor("us", [MLP, D // 32], F32, kind="ExternalInput").ap()
    dp_h = nc.dram_tensor("dp", [D, MLP // 2], I32, kind="ExternalInput").ap()
    ds_h = nc.dram_tensor("ds", [D, MLP // 32], F32, kind="ExternalInput").ap()
    out_h = nc.dram_tensor("out", [NTOK, D], F32, kind="ExternalOutput").ap()

    ikind = "ExternalOutput" if debug else "Internal"
    wq16 = nc.dram_tensor("wq16", [D, D], BF16, kind="Internal").ap()
    wg16 = nc.dram_tensor("wg16", [D, D], BF16, kind="Internal").ap()
    wo16 = nc.dram_tensor("wo16", [D, D], BF16, kind="Internal").ap()
    k16 = nc.dram_tensor("k16", [M, D], BF16, kind="Internal").ap()
    wqT_t = nc.dram_tensor("wqT_t", [EC, P, EC, P], BF16, kind="Internal").ap()
    wgT_t = nc.dram_tensor("wgT_t", [EC, P, EC, P], BF16, kind="Internal").ap()
    woT_t = nc.dram_tensor("woT_t", [4, 2, P, 8, 512], BF16, kind="Internal").ap()
    kT = nc.dram_tensor("kT", [D, M], BF16, kind="Internal").ap()
    natg = nc.dram_tensor("natg", [MLP, D], BF16, kind="Internal").ap()
    natu = nc.dram_tensor("natu", [MLP, D], BF16, kind="Internal").ap()
    natd = nc.dram_tensor("natd", [D, MLP], BF16, kind="Internal").ap()
    wgT_f = nc.dram_tensor("wgT_f", [MLP // P, P, EC, P], BF16, kind="Internal").ap()
    wuT_f = nc.dram_tensor("wuT_f", [MLP // P, P, EC, P], BF16, kind="Internal").ap()
    wdT = nc.dram_tensor("wdT", [MLP, D], BF16, kind="Internal").ap()
    xls = nc.dram_tensor("xls", [NTOK, D], BF16, kind=ikind).ap()
    yls = nc.dram_tensor("yls", [NTOK, D], BF16, kind="Internal").ap()
    ys = nc.dram_tensor("ys", [NTOK, D], F32, kind=ikind).ap()

    k = K(nc)
    es = contextlib.ExitStack()
    sbuf = nc.sbuf_tensor
    psum = nc.psum_tensor

    stt = [es.enter_context(sbuf(f"stt{i}", [P, 32], F32)) for i in range(2)]
    b_stt = [Buf() for _ in range(2)]
    stg16 = [es.enter_context(sbuf("stg16_0", [P, 4096], BF16)),
             es.enter_context(sbuf("stg16_1", [P, 2048], BF16))]
    b_stg16 = [Buf() for _ in range(2)]
    xstg = [es.enter_context(sbuf(f"xstg{i}", [P, D], F32)) for i in range(2)]
    b_xstg = [Buf() for _ in range(2)]
    lstg = [es.enter_context(sbuf(f"lstg{i}", [P, D], BF16)) for i in range(2)]
    b_lstg = [Buf() for _ in range(2)]
    xln = es.enter_context(sbuf("xlnf", [P, EC * TA], BF16))
    qf = es.enter_context(sbuf("qf", [P, H * TA], BF16))
    gf = es.enter_context(sbuf("gf", [P, H * TA], BF16))
    ogf = es.enter_context(sbuf("ogf", [P, H * TA], BF16))
    b_xln = [Buf() for _ in range(EC)]
    b_qf, b_gf, b_ogf = Buf(), Buf(), Buf()
    wsl = [es.enter_context(sbuf(f"wsl{i}", [P, EC * P], BF16)) for i in range(2)]
    b_wsl = [Buf() for _ in range(2)]
    wosl = [es.enter_context(sbuf(f"wosl{i}", [P, 8 * 512], BF16)) for i in range(2)]
    b_wosl = [Buf() for _ in range(2)]
    ktsl = [es.enter_context(sbuf(f"ktsl{i}", [P, M], BF16)) for i in range(2)]
    b_ktsl = [Buf() for _ in range(2)]
    vsl = [es.enter_context(sbuf(f"vsl{i}", [P, 32 * DK], BF16)) for i in range(2)]
    b_vsl = [[Buf() for _ in range(4)] for _ in range(2)]
    ensb = [es.enter_context(sbuf(f"ensb{i}", [P, TA], BF16)) for i in range(6)]
    b_ensb = [Buf() for _ in range(6)]
    ypc = [es.enter_context(sbuf(f"ypc{i}", [P, 512], F32)) for i in range(4)]
    b_ypc = [Buf() for _ in range(4)]
    xpc = [es.enter_context(sbuf(f"xpc{i}", [P, 512], F32)) for i in range(2)]
    b_xpc = [Buf() for _ in range(2)]
    dvd_sh = es.enter_context(sbuf("dvd", [P, 2048], BF16))
    b_dvd_sh = Buf()
    dec = []
    for i in range(2):
        dec.append(dict(
            u8=es.enter_context(sbuf(f"du8_{i}", [P, 1024], U8)),
            m=es.enter_context(sbuf(f"dm_{i}", [P, 2048], I16)),
            s=es.enter_context(sbuf(f"dsx_{i}", [P, 2048], I16)),
            c=es.enter_context(sbuf(f"dcx_{i}", [P, 2048], I16)),
            vd=dvd_sh, sc=es.enter_context(sbuf(f"dsc_{i}", [P, 64], F32)),
            sc16=es.enter_context(sbuf(f"dsc16_{i}", [P, 64], BF16)),
            b_u8=Buf(), b_m=Buf(), b_s=Buf(), b_c=Buf(), b_vd=b_dvd_sh,
            b_sc=Buf(), b_sc16=Buf(),
        ))


    p_en = [es.enter_context(psum(f"p_en{i}", [P, 512], F32)) for i in range(2)]
    b_pen = [Buf() for _ in range(2)]
    p_out = [es.enter_context(psum(f"p_out{i}", [P, 512], F32)) for i in range(2)]
    b_pout = [Buf() for _ in range(2)]
    p_mm = [es.enter_context(psum(f"p_mm{i}", [P, 512], F32)) for i in range(4)]
    b_pmm = [Buf() for _ in range(4)]

    b_wq16, b_wg16, b_wo16, b_k16 = Buf(), Buf(), Buf(), Buf()
    b_wqT, b_wgT, b_woT, b_kT = Buf(), Buf(), Buf(), Buf()
    b_natg = [Buf() for _ in range(MLP // P)]
    b_natu = [Buf() for _ in range(MLP // P)]
    b_natd = [Buf() for _ in range(128)]
    b_wgTf, b_wuTf, b_wdT = Buf(), Buf(), Buf()
    b_xls = [[Buf() for _ in range(4)] for _ in range(NT_A)]
    b_yls = [[Buf() for _ in range(4)] for _ in range(NT_A)]
    b_ys = [[Buf() for _ in range(4)] for _ in range(32)]

    def cast_to_16(src, dst, nrow, b_dst, tag):
        for c0 in range(nrow // P):
            sl = c0 % 2
            k.dma("gpsimd", f"stg{sl}",
                  lambda g, sl=sl, c0=c0, src=src:
                  g.dma_start(out=stg16[sl][:, 0:D],
                              in_=src[c0 * P:(c0 + 1) * P, :]),
                  writes=(b_stg16[sl],))
            k.dma("sync", f"stgw{sl}",
                  lambda s, sl=sl, c0=c0, dst=dst:
                  s.dma_start(out=dst[c0 * P:(c0 + 1) * P, :],
                              in_=stg16[sl][:, 0:D]),
                  reads=(b_stg16[sl],), writes=(b_dst,))

    cast_to_16(wq_h, wq16, D, b_wq16, "wq")
    cast_to_16(wg_h, wg16, D, b_wg16, "wg")
    cast_to_16(wo_h, wo16, D, b_wo16, "wo")
    cast_to_16(wk_h, k16, M, b_k16, "wk")

    def xbar_tiled(src16, b_src, dstT, b_dst):
        for ec in range(EC):
            sl = ec % 2
            k.dma("sync", f"stg{sl}",
                  lambda s, ec=ec, sl=sl, src16=src16:
                  s.dma_start_transpose(stg16[sl][:, 0:D], src16[:, ec * P:(ec + 1) * P]),
                  reads=(b_src,), writes=(b_stg16[sl],))
            for half in range(2):
                k.dma("sync", f"stgw{sl}",
                      lambda s, ec=ec, sl=sl, half=half, dstT=dstT:
                      s.dma_start(out=dstT[half * 8:(half + 1) * 8, :, ec].transpose([1, 0, 2]),
                                  in_=stg16[sl][:, half * 1024:(half + 1) * 1024]
                                  .rearrange("p (fc col) -> p fc col", col=P)),
                      reads=(b_stg16[sl],), writes=(b_dst,))

    xbar_tiled(wq16, b_wq16, wqT_t, b_wqT)
    xbar_tiled(wg16, b_wg16, wgT_t, b_wgT)

    for vc in range(EC):
        sl = vc % 2
        k.dma("sync", f"stg{sl}",
              lambda s, vc=vc, sl=sl:
              s.dma_start_transpose(stg16[sl][:, 0:D], wo16[:, vc * P:(vc + 1) * P]),
              reads=(b_wo16,), writes=(b_stg16[sl],))
        for eq in range(4):
            k.dma("sync", f"stgw{sl}",
                  lambda s, vc=vc, sl=sl, eq=eq:
                  s.dma_start(out=woT_t[eq, vc // 8, :, vc % 8],
                              in_=stg16[sl][:, eq * 512:(eq + 1) * 512]),
                  reads=(b_stg16[sl],), writes=(b_woT,))

    for c in range(EC):
        sl = 0
        k.dma("sync", f"stg{sl}",
              lambda s, c=c, sl=sl:
              s.dma_start_transpose(stg16[sl][:, 0:M], k16[:, c * P:(c + 1) * P]),
              reads=(b_k16,), writes=(b_stg16[sl],))
        k.dma("sync", f"stgw{sl}",
              lambda s, c=c, sl=sl:
              s.dma_start(out=kT.rearrange("(c p) m -> c p m", p=P)[c],
                          in_=stg16[sl][:, 0:M]),
              reads=(b_stg16[sl],), writes=(b_kT,))

    def decode_subtile(tail_eng, pack_ap, sc_ap, nat_out_ap, b_nat, slot):
        d = dec[slot]
        k.dma("gpsimd", f"du8{slot}",
              lambda g, pack_ap=pack_ap, d=d:
              g.dma_start(out=d["u8"][:, :], in_=pack_ap),
              writes=(d["b_u8"],))
        k.dma("gpsimd", f"dsc{slot}",
              lambda g, sc_ap=sc_ap, d=d:
              g.dma_start(out=d["sc"][:, :], in_=sc_ap),
              writes=(d["b_sc"],))
        w16 = d["u8"].ap().bitcast(I16)
        for j in range(4):
            if j == 0:
                k.op("vector", lambda e, d=d, w16=w16:
                     e.tensor_single_scalar(out=d["m"][:, 0:512], in_=w16, scalar=7,
                                            op=ALU.bitwise_and),
                     reads=(d["b_u8"],), writes=(d["b_m"],))
            else:
                k.op("vector", lambda e, d=d, w16=w16, j=j:
                     e.tensor_scalar(out=d["m"][:, j * 512:(j + 1) * 512], in0=w16,
                                     scalar1=4 * j, scalar2=7,
                                     op0=ALU.logical_shift_right, op1=ALU.bitwise_and),
                     reads=(d["b_u8"],), writes=(d["b_m"],))
        for j in range(4):
            k.op("vector", lambda e, d=d, w16=w16, j=j:
                 e.tensor_scalar(out=d["s"][:, j * 512:(j + 1) * 512], in0=w16,
                                 scalar1=8 << (4 * j), scalar2=12 - 4 * j,
                                 op0=ALU.bitwise_and, op1=ALU.logical_shift_left),
                 reads=(d["b_u8"],), writes=(d["b_s"],))
        k.op("vector", lambda e, d=d:
             e.tensor_single_scalar(out=d["c"][:, :], in_=d["m"][:, :], scalar=2, op=ALU.min),
             reads=(d["b_m"],), writes=(d["b_c"],))
        k.op("vector", lambda e, d=d:
             e.tensor_scalar(out=d["m"][:, :], in0=d["m"][:, :], scalar1=252, scalar2=254,
                             op0=ALU.add, op1=ALU.max),
             reads=(d["b_m"],), writes=(d["b_m"],))
        k.op("vector", lambda e, d=d:
             e.tensor_single_scalar(out=d["m"][:, :], in_=d["m"][:, :], scalar=6,
                                    op=ALU.logical_shift_left),
             reads=(d["b_m"],), writes=(d["b_m"],))
        k.op("vector", lambda e, d=d:
             e.tensor_tensor(out=d["m"][:, :], in0=d["m"][:, :], in1=d["s"][:, :],
                             op=ALU.add),
             reads=(d["b_m"], d["b_s"]), writes=(d["b_m"],))
        corrf = d["s"].ap().bitcast(BF16)
        k.op(tail_eng, lambda e, d=d, corrf=corrf:
             e.tensor_single_scalar(out=corrf, in_=d["c"][:, :], scalar=0.5, op=ALU.mult),
             reads=(d["b_c"],), writes=(d["b_s"],))
        val = d["c"].ap().bitcast(BF16)
        k.op(tail_eng, lambda e, d=d, corrf=corrf, val=val:
             e.tensor_mul(out=val, in0=d["m"].ap().bitcast(BF16), in1=corrf),
             reads=(d["b_m"], d["b_s"]), writes=(d["b_c"],))
        k.op(tail_eng, lambda e, d=d:
             e.tensor_copy(out=d["sc16"][:, :], in_=d["sc"][:, :]),
             reads=(d["b_sc"],), writes=(d["b_sc16"],))
        vdr = d["vd"].ap().rearrange("p (g four) -> p four g", four=4)
        for j in range(4):
            k.op(tail_eng, lambda e, d=d, val=val, vdr=vdr, j=j:
                 e.tensor_tensor(
                     out=vdr[:, j, :].rearrange("p (gb gg) -> p gb gg", gg=8),
                     in0=val[:, j * 512:(j + 1) * 512].rearrange("p (gb gg) -> p gb gg", gg=8),
                     in1=d["sc16"].ap().unsqueeze(2).to_broadcast([P, 64, 8]),
                     op=ALU.mult),
                 reads=(d["b_c"], d["b_sc16"]), writes=(d["b_vd"],))
        k.dma("gpsimd", f"dnw{slot}",
              lambda g, nat_out_ap=nat_out_ap, d=d:
              g.dma_start(out=nat_out_ap, in_=d["vd"][:, :]),
              reads=(d["b_vd"],), writes=(b_nat,))

    decode_jobs = []
    for j in range(MLP // P):
        r = j
        decode_jobs.append((gp_h[r * P:(r + 1) * P, :], gs_h[r * P:(r + 1) * P, :],
                            natg[r * P:(r + 1) * P, :], b_natg[r]))
        decode_jobs.append((up_h[r * P:(r + 1) * P, :], us_h[r * P:(r + 1) * P, :],
                            natu[r * P:(r + 1) * P, :], b_natu[r]))
        s8, r2 = j // 16, j % 16
        decode_jobs.append((dp_h[r2 * P:(r2 + 1) * P, s8 * 1024:(s8 + 1) * 1024],
                            ds_h[r2 * P:(r2 + 1) * P, s8 * 64:(s8 + 1) * 64],
                            natd[r2 * P:(r2 + 1) * P, s8 * 2048:(s8 + 1) * 2048],
                            b_natd[r2 * 8 + s8]))
    n_jobs = len(decode_jobs)
    djobs = [("vector", decode_jobs[i]) for i in range(n_jobs)]
    dec_state = {"i": 0}

    def emit_decode(n):
        stop = min(dec_state["i"] + n, n_jobs)
        while dec_state["i"] < stop:
            i = dec_state["i"]
            eng, (pa, sa, na, bn) = djobs[i]
            decode_subtile(eng, pa, sa, na, bn, i % 2)
            dec_state["i"] += 1

    def emit_ffn_transposes(idx):
        G = idx
        for nat, b_nat, dstT, b_dst in ((natg, b_natg, wgT_f, b_wgTf),
                                        (natu, b_natu, wuT_f, b_wuTf)):
            rdeps = tuple(b_nat[G * 16 + i] for i in range(16))
            for ec in range(EC):
                sl = ec % 2
                k.dma("sync", f"stg{sl}",
                      lambda s, nat=nat, G=G, ec=ec, sl=sl:
                      s.dma_start_transpose(
                          stg16[sl][:, 0:2048],
                          nat[G * 2048:(G + 1) * 2048, ec * P:(ec + 1) * P]),
                      reads=rdeps, writes=(b_stg16[sl],))
                for half in range(2):
                    k.dma("sync", f"stgw{sl}",
                          lambda s, dstT=dstT, G=G, ec=ec, sl=sl, half=half:
                          s.dma_start(
                              out=dstT[G * 16 + half * 8:G * 16 + (half + 1) * 8, :, ec]
                              .transpose([1, 0, 2]),
                              in_=stg16[sl][:, half * 1024:(half + 1) * 1024]
                              .rearrange("p (mc col) -> p mc col", col=P)),
                          reads=(b_stg16[sl],), writes=(b_dst,))
        for mc in range(idx * 16, (idx + 1) * 16):
            sl = mc % 2
            rdeps = tuple(b_natd[r * 8 + (mc // 16)] for r in range(16))
            k.dma("sync", f"stg{sl}",
                  lambda s, mc=mc, sl=sl:
                  s.dma_start_transpose(stg16[sl][:, 0:D], natd[:, mc * P:(mc + 1) * P]),
                  reads=rdeps, writes=(b_stg16[sl],))
            k.dma("sync", f"stgw{sl}",
                  lambda s, mc=mc, sl=sl:
                  s.dma_start(out=wdT.rearrange("(mc p) h -> mc p h", p=P)[mc],
                              in_=stg16[sl][:, 0:D]),
                  reads=(b_stg16[sl],), writes=(b_wdT,))

    def ln_chunk(xb, b_xb, outb, b_outb, si, save_slot=None):
        st = stt[si]
        b_st = b_stt[si]
        for g4 in range(4):
            k.op("vector", lambda v, g4=g4, xb=xb, st=st:
                 v.bn_stats(out=st[:, g4 * 6:(g4 + 1) * 6],
                            in_=xb[:, g4 * 512:(g4 + 1) * 512]),
                 reads=(b_xb,), writes=(b_st,))
        k.op("vector", lambda v, st=st: v.bn_aggr(out=st[:, 24:26], in_=st[:, 0:24]),
             reads=(b_st,), writes=(b_st,))
        k.op("vector", lambda v, st=st:
             v.tensor_scalar_add(out=st[:, 25:26], in0=st[:, 25:26], scalar1=1e-5),
             reads=(b_st,), writes=(b_st,))
        k.op("scalar", lambda s, st=st:
             s.activation(out=st[:, 31:32], in_=st[:, 25:26], func=AF.Sqrt),
             reads=(b_st,), writes=(b_st,))
        k.op("vector", lambda v, st=st: v.reciprocal(out=st[:, 26:27], in_=st[:, 31:32]),
             reads=(b_st,), writes=(b_st,))
        k.op("vector", lambda v, st=st:
             v.tensor_mul(out=st[:, 27:28], in0=st[:, 25:26], in1=st[:, 26:27]),
             reads=(b_st,), writes=(b_st,))
        k.op("vector", lambda v, st=st:
             v.tensor_mul(out=st[:, 28:29], in0=st[:, 27:28], in1=st[:, 26:27]),
             reads=(b_st,), writes=(b_st,))
        k.op("vector", lambda v, st=st:
             v.tensor_scalar(out=st[:, 28:29], in0=st[:, 28:29], scalar1=-0.5, scalar2=1.5,
                             op0=ALU.mult, op1=ALU.add),
             reads=(b_st,), writes=(b_st,))
        k.op("vector", lambda v, st=st:
             v.tensor_mul(out=st[:, 29:30], in0=st[:, 26:27], in1=st[:, 28:29]),
             reads=(b_st,), writes=(b_st,))
        k.op("vector", lambda v, st=st:
             v.scalar_tensor_tensor(out=st[:, 30:31], in0=st[:, 24:25], scalar=-1.0,
                                    in1=st[:, 29:30], op0=ALU.mult, op1=ALU.mult),
             reads=(b_st,), writes=(b_st,))
        k.op("scalar", lambda s, xb=xb, outb=outb, st=st:
             s.activation(out=outb, in_=xb, func=AF.Identity,
                          scale=st[:, 29:30], bias=st[:, 30:31]),
             reads=(b_xb, b_st), writes=(b_outb,))

    def phase_a_tile(t):
        t0 = t * TA
        for tc in range(4):
            sl = tc % 2
            k.dma("sync", f"xst{sl}",
                  lambda s, t0=t0, tc=tc, sl=sl:
                  s.dma_start(out=xstg[sl][:, :], in_=x_h[t0 + tc * P:t0 + (tc + 1) * P, :]),
                  writes=(b_xstg[sl],))
            ln_chunk(xstg[sl][:, :], b_xstg[sl], lstg[sl][:, :], b_lstg[sl], sl)
            k.dma("sync", f"lst{sl}",
                  lambda s, t0=t0, tc=tc, sl=sl:
                  s.dma_start(out=xls[t0 + tc * P:t0 + (tc + 1) * P, :], in_=lstg[sl][:, :]),
                  reads=(b_lstg[sl],), writes=(b_xls[t][tc],))
        xt_last = None
        for ec in range(EC):
            xt_last = k.dma("sync", "xbar",
                            lambda s, t0=t0, ec=ec:
                            s.dma_start_transpose(xln[:, ec * TA:(ec + 1) * TA],
                                                  xls[t0:t0 + TA, ec * P:(ec + 1) * P]),
                            reads=tuple(b_xls[t]), writes=(b_xln[ec],))
        for ec in range(EC):
            _merge(b_xln[ec].w, {xt_last[0]: xt_last[1]})
        for wT, b_wT, dstbuf, b_dst, act in ((wqT_t, b_wqT, qf, b_qf, None),
                                             (wgT_t, b_wgT, gf, b_gf, AF.Silu)):
            for fc in range(EC):
                sl = fc % 2
                k.dma("sync", f"wsl{sl}",
                      lambda sy, wT=wT, fc=fc, sl=sl:
                      sy.dma_start(out=wsl[sl][:, :].rearrange("p (ec col) -> p ec col", col=P),
                                  in_=wT[fc]),
                      reads=(b_wT,), writes=(b_wsl[sl],))
                pm = fc % 4
                for ecc in range(EC):
                    k.op("tensor", lambda te, sl=sl, ecc=ecc, pm=pm:
                         te.matmul(out=p_mm[pm][:, :],
                                   lhsT=wsl[sl][:, ecc * P:(ecc + 1) * P],
                                   rhs=xln[:, ecc * TA:ecc * TA + TA],
                                   start=(ecc == 0), stop=(ecc == EC - 1)),
                         reads=(b_wsl[sl], b_xln[ecc]), writes=(b_pmm[pm],))
                if act is None:
                    k.op("vector", lambda v, fc=fc, pm=pm, dstbuf=dstbuf:
                         v.tensor_copy(out=dstbuf[:, fc * TA:(fc + 1) * TA],
                                       in_=p_mm[pm][:, :]),
                         reads=(b_pmm[pm],), writes=(b_dst,))
                else:
                    k.op("scalar", lambda s, fc=fc, pm=pm, dstbuf=dstbuf, act=act:
                         s.activation(out=dstbuf[:, fc * TA:(fc + 1) * TA],
                                      in_=p_mm[pm][:, :], func=act),
                         reads=(b_pmm[pm],), writes=(b_dst,))
        for h in range(H):
            sl = h % 2
            k.dma("sync", f"kts{sl}",
                  lambda s, h=h, sl=sl:
                  s.dma_start(out=ktsl[sl][:, :],
                              in_=kT.rearrange("(c p) m -> c p m", p=P)[h]),
                  reads=(b_kT,), writes=(b_ktsl[sl],))
            vt = None
            for q4 in range(4):
                vt = k.dma("gpsimd", f"vsl{sl}",
                           lambda g, h=h, sl=sl, q4=q4:
                           g.dma_start(
                               out=vsl[sl][:, q4 * 1024:(q4 + 1) * 1024]
                               .rearrange("p (c dd) -> p c dd", dd=DK),
                               in_=wv_h.rearrange("(c p) e -> p c e", p=P)
                               [:, q4 * 8:(q4 + 1) * 8, h * DK:(h + 1) * DK]),
                           writes=(b_vsl[sl][q4],))
            for q4 in range(4):
                _merge(b_vsl[sl][q4].w, {vt[0]: vt[1]})
            po = h % 2
            en_pool = [p_en[0], p_en[1], p_mm[0], p_mm[1], p_mm[2], p_mm[3]]
            b_en_pool = [b_pen[0], b_pen[1], b_pmm[0], b_pmm[1], b_pmm[2], b_pmm[3]]
            for cc in range(36):
                if cc < 32:
                    c, pes = cc, cc % 6
                    k.op("tensor", lambda te, sl=sl, c=c, pes=pes, h=h:
                         te.matmul(out=en_pool[pes][:, :],
                                   lhsT=ktsl[sl][:, c * P:(c + 1) * P],
                                   rhs=qf[:, h * TA:(h + 1) * TA],
                                   start=True, stop=True),
                         reads=(b_ktsl[sl], b_qf), writes=(b_en_pool[pes],))
                    k.op("scalar", lambda s, pes=pes:
                         s.activation(out=ensb[pes][:, :], in_=en_pool[pes][:, :],
                                      func=AF.Silu, scale=SQD),
                         reads=(b_en_pool[pes],), writes=(b_ensb[pes],))
                if cc >= 4:
                    c, pes = cc - 4, (cc - 4) % 6
                    k.op("tensor", lambda te, sl=sl, c=c, pes=pes, po=po:
                         te.matmul(out=p_out[po][:, :],
                                   lhsT=vsl[sl][:, c * DK:(c + 1) * DK],
                                   rhs=ensb[pes][:, :],
                                   start=(c == 0), stop=(c == 31)),
                         reads=(b_vsl[sl][c // 8], b_ensb[pes]), writes=(b_pout[po],))
            k.op("vector", lambda v, h=h, po=po:
                 v.scalar_tensor_tensor(out=ogf[:, h * TA:(h + 1) * TA],
                                        in0=p_out[po][:, :], scalar=SQD,
                                        in1=gf[:, h * TA:(h + 1) * TA],
                                        op0=ALU.mult, op1=ALU.mult),
                 reads=(b_pout[po], b_gf), writes=(b_ogf,))
            emit_decode(3)
        for eq in range(4):
            for vh in range(2):
                sl = (eq * 2 + vh) % 2
                k.dma("sync", f"wos{sl}",
                      lambda sy, eq=eq, vh=vh, sl=sl:
                      sy.dma_start(out=wosl[sl][:, :]
                                  .rearrange("p (vc col) -> p vc col", col=512),
                                  in_=woT_t[eq, vh]),
                      reads=(b_woT,), writes=(b_wosl[sl],))
                for ts in range(4):
                    for vc8 in range(8):
                        vc = vh * 8 + vc8
                        k.op("tensor",
                             lambda te, ts=ts, vc=vc, vc8=vc8, sl=sl, vh=vh:
                             te.matmul(out=p_mm[ts][:, :],
                                       lhsT=ogf[:, vc * TA + ts * P:vc * TA + (ts + 1) * P],
                                       rhs=wosl[sl][:, vc8 * 512:(vc8 + 1) * 512],
                                       start=(vh == 0 and vc8 == 0),
                                       stop=(vh == 1 and vc8 == 7)),
                             reads=(b_ogf, b_wosl[sl]), writes=(b_pmm[ts],))
            for ts in range(4):
                xsl = ts % 2
                k.dma("sync", f"xpc{xsl}",
                      lambda s, t0=t0, eq=eq, ts=ts, xsl=xsl:
                      s.dma_start(out=xpc[xsl][:, :],
                                  in_=x_h[t0 + ts * P:t0 + (ts + 1) * P,
                                          eq * 512:(eq + 1) * 512]),
                      writes=(b_xpc[xsl],))
                k.op("vector", lambda v, ts=ts, xsl=xsl:
                     v.tensor_add(out=ypc[ts][:, :], in0=p_mm[ts][:, :], in1=xpc[xsl][:, :]),
                     reads=(b_pmm[ts], b_xpc[xsl]), writes=(b_ypc[ts],))
                k.dma("sync", f"ypc{ts}",
                      lambda s, t0=t0, eq=eq, ts=ts:
                      s.dma_start(out=ys[t0 + ts * P:t0 + (ts + 1) * P,
                                         eq * 512:(eq + 1) * 512],
                                  in_=ypc[ts][:, :]),
                      reads=(b_ypc[ts],), writes=(b_ys[t * 4 + ts][eq],))
        for tc in range(4):
            sl = tc % 2
            k.dma("sync", f"xst{sl}",
                  lambda s, t0=t0, tc=tc, sl=sl:
                  s.dma_start(out=xstg[sl][:, :], in_=ys[t0 + tc * P:t0 + (tc + 1) * P, :]),
                  reads=tuple(b_ys[t * 4 + tc]), writes=(b_xstg[sl],))
            ln_chunk(xstg[sl][:, :], b_xstg[sl], lstg[sl][:, :], b_lstg[sl], sl)
            k.dma("sync", f"lst{sl}",
                  lambda s, t0=t0, tc=tc, sl=sl:
                  s.dma_start(out=yls[t0 + tc * P:t0 + (tc + 1) * P, :], in_=lstg[sl][:, :]),
                  reads=(b_lstg[sl],), writes=(b_yls[t][tc],))

    for t in range(NT_A):
        phase_a_tile(t)
        if t >= 1:
            emit_ffn_transposes(t - 1)
    emit_decode(n_jobs)
    emit_ffn_transposes(7)

    bar = k.barrier_toks()

    def buf_after():
        return Buf(bar)

    yln = xln
    b_yln = [buf_after() for _ in range(EC)]
    inter = qf
    b_inter = buf_after()
    accA = gf.ap().bitcast(F32)
    accB = ogf.ap().bitcast(F32)
    acc_ap = [accA[:, 0:D], accA[:, D:2 * D], accB[:, 0:D], accB[:, D:2 * D]]
    b_acc = [buf_after() for _ in range(4)]
    ystg = [xstg[i].ap() for i in range(2)]
    b_ystg = [buf_after() for _ in range(2)]
    ostg = [stg16[0].ap().bitcast(F32), stg16[0].ap().bitcast(F32)]
    b_ostg0 = buf_after()
    b_ostg = [b_ostg0, b_ostg0]
    wgsl = [ktsl[i].ap() for i in range(2)]
    b_wgsl = [buf_after() for _ in range(2)]
    wusl = [vsl[i].ap() for i in range(2)]
    b_wusl = [buf_after() for _ in range(2)]
    wdsl = [wosl[0].ap(), wosl[1].ap()]
    b_wdsl = [buf_after() for _ in range(2)]
    glt = [ypc[0], ypc[1], ypc[2], ypc[3], xpc[0]]
    b_glt = [buf_after() for _ in range(5)]
    p_dn = [p_en[0].ap(), p_en[1].ap(), p_out[0].ap(), p_out[1].ap()]
    b_pdn = [buf_after() for _ in range(4)]
    b_pgu = [buf_after() for _ in range(4)]

    def phase_b_tile(b):
        t0 = b * TB
        yt_last = None
        for ec in range(EC):
            yt_last = k.dma("sync", "xbar",
                            lambda s, t0=t0, ec=ec:
                            s.dma_start_transpose(yln[:, ec * TB:(ec + 1) * TB],
                                                  yls[t0:t0 + TB, ec * P:(ec + 1) * P]),
                            reads=tuple(b_yls[b]), writes=(b_yln[ec],))
        for ec in range(EC):
            _merge(b_yln[ec].w, {yt_last[0]: yt_last[1]})
        for G in range(8):
            for mcl in range(16):
                mc = G * 16 + mcl
                sl = (mcl // 2) % 2
                half = (mcl % 2) * 2048
                pg = mcl % 2
                pu = 2 + mcl % 2
                if mcl % 2 == 0:
                    k.dma("sync", f"wgl{sl}",
                          lambda sy, mc=mc, sl=sl:
                          sy.dma_start(
                              out=wgsl[sl].rearrange("p (two n) -> p two n", n=2048),
                              in_=wgT_f[mc:mc + 2].rearrange("mc p ec col -> mc p (ec col)")
                              .transpose([1, 0, 2])),
                          reads=(b_wgTf,), writes=(b_wgsl[sl],))
                    k.dma("sync", f"wul{sl}",
                          lambda sy, mc=mc, sl=sl:
                          sy.dma_start(
                              out=wusl[sl].rearrange("p (two n) -> p two n", n=2048),
                              in_=wuT_f[mc:mc + 2].rearrange("mc p ec col -> mc p (ec col)")
                              .transpose([1, 0, 2])),
                          reads=(b_wuTf,), writes=(b_wusl[sl],))
                for ecc in range(EC):
                    k.op("tensor", lambda te, sl=sl, ecc=ecc, pg=pg, half=half:
                         te.matmul(out=p_mm[pg][:, :],
                                   lhsT=wgsl[sl][:, half + ecc * P:half + (ecc + 1) * P],
                                   rhs=yln[:, ecc * TB:(ecc + 1) * TB],
                                   start=(ecc == 0), stop=(ecc == EC - 1)),
                         reads=(b_wgsl[sl], b_yln[ecc]), writes=(b_pgu[pg],))
                for ecc in range(EC):
                    k.op("tensor", lambda te, sl=sl, ecc=ecc, pu=pu, half=half:
                         te.matmul(out=p_mm[pu][:, :],
                                   lhsT=wusl[sl][:, half + ecc * P:half + (ecc + 1) * P],
                                   rhs=yln[:, ecc * TB:(ecc + 1) * TB],
                                   start=(ecc == 0), stop=(ecc == EC - 1)),
                         reads=(b_wusl[sl], b_yln[ecc]), writes=(b_pgu[pu],))
                k.op("scalar", lambda s, pg=pg:
                     s.activation(out=glt[0][:, :], in_=p_mm[pg][:, :], func=AF.Square),
                     reads=(b_pgu[pg],), writes=(b_glt[0],))
                k.op("vector", lambda v:
                     v.tensor_scalar(out=glt[1][:, :], in0=glt[0][:, :], scalar1=GELU_A,
                                     scalar2=1.0, op0=ALU.mult, op1=ALU.add),
                     reads=(b_glt[0],), writes=(b_glt[1],))
                k.op("vector", lambda v, pg=pg:
                     v.tensor_mul(out=glt[2][:, :], in0=p_mm[pg][:, :], in1=glt[1][:, :]),
                     reads=(b_pgu[pg], b_glt[1]), writes=(b_glt[2],))
                k.op("scalar", lambda s:
                     s.activation(out=glt[3][:, :], in_=glt[2][:, :], func=AF.Sigmoid,
                                  scale=GELU_S),
                     reads=(b_glt[2],), writes=(b_glt[3],))
                k.op("vector", lambda v, pg=pg:
                     v.tensor_mul(out=glt[4][:, :], in0=p_mm[pg][:, :], in1=glt[3][:, :]),
                     reads=(b_pgu[pg], b_glt[3]), writes=(b_glt[4],))
                k.op("vector", lambda v, pu=pu, mcl=mcl:
                     v.tensor_mul(out=inter[:, mcl * TB:(mcl + 1) * TB],
                                  in0=glt[4][:, :], in1=p_mm[pu][:, :]),
                     reads=(b_glt[4], b_pgu[pu]), writes=(b_inter,))
            for hq in range(4):
                for half in range(2):
                    k.dma("sync", f"wdl{half}",
                          lambda s, G=G, hq=hq, half=half:
                          s.dma_start(
                              out=wdsl[half].rearrange("p (mc n) -> p mc n", n=512),
                              in_=wdT[(G * 16 + half * 8) * P:(G * 16 + (half + 1) * 8) * P,
                                      hq * 512:(hq + 1) * 512]
                              .rearrange("(mc p) n -> p mc n", p=P)),
                          reads=(b_wdT,), writes=(b_wdsl[half],))
                for mcl in range(16):
                    sl = mcl // 8
                    off = (mcl % 8) * 512
                    for ts in range(4):
                        k.op("tensor", lambda te, mcl=mcl, ts=ts, sl=sl, off=off:
                             te.matmul(out=p_dn[ts],
                                       lhsT=inter[:, mcl * TB + ts * P:mcl * TB + (ts + 1) * P],
                                       rhs=wdsl[sl][:, off:off + 512],
                                       start=(mcl == 0), stop=(mcl == 15)),
                             reads=(b_inter, b_wdsl[sl]), writes=(b_pdn[ts],))
                for ts in range(4):
                    if G == 0:
                        k.op("vector", lambda v, ts=ts, hq=hq:
                             v.tensor_copy(out=acc_ap[ts][:, hq * 512:(hq + 1) * 512],
                                           in_=p_dn[ts]),
                             reads=(b_pdn[ts],), writes=(b_acc[ts],))
                    else:
                        k.op("vector", lambda v, ts=ts, hq=hq:
                             v.tensor_add(out=acc_ap[ts][:, hq * 512:(hq + 1) * 512],
                                          in0=acc_ap[ts][:, hq * 512:(hq + 1) * 512],
                                          in1=p_dn[ts]),
                             reads=(b_pdn[ts], b_acc[ts]), writes=(b_acc[ts],))
        for ts in range(4):
            sl = ts % 2
            k.dma("sync", f"yst{sl}",
                  lambda s, t0=t0, ts=ts, sl=sl:
                  s.dma_start(out=ystg[sl], in_=ys[t0 + ts * P:t0 + (ts + 1) * P, :]),
                  reads=tuple(b_ys[b * 4 + ts]), writes=(b_ystg[sl],))
            k.op("vector", lambda v, ts=ts, sl=sl:
                 v.tensor_add(out=ostg[sl], in0=acc_ap[ts][:, :], in1=ystg[sl]),
                 reads=(b_acc[ts], b_ystg[sl]), writes=(b_ostg[sl],))
            k.dma("sync", f"ost{sl}",
                  lambda s, t0=t0, ts=ts, sl=sl:
                  s.dma_start(out=out_h[t0 + ts * P:t0 + (ts + 1) * P, :], in_=ostg[sl]),
                  reads=(b_ostg[sl],), writes=())

    for b in range(NT_B):
        phase_b_tile(b)

    fin = [(("q", "ost0"), k.qcnt.get("ost0", 0)), (("q", "ost1"), k.qcnt.get("ost1", 0))]
    k.op("sync", lambda s: s.nop(), extra=fin)
    k.op("gpsimd", lambda g: g.nop(), extra=fin)

    k.emit(es)
    return nc, es


def kernel(L0_attn_x, L0_attn_Wq, L0_attn_k_weight, L0_attn_v_weight, L0_attn_Wg,
           L0_attn_Wout, L0_ffn_gate_W_packed, L0_ffn_gate_scales, L0_ffn_up_W_packed,
           L0_ffn_up_scales, L0_ffn_down_W_packed, L0_ffn_down_scales):
    x = np.ascontiguousarray(np.asarray(L0_attn_x, dtype=np.float32))
    weights = dict(
        wq=np.ascontiguousarray(np.asarray(L0_attn_Wq, dtype=np.float32)),
        wk=np.ascontiguousarray(np.asarray(L0_attn_k_weight, dtype=np.float32)),
        wv=np.ascontiguousarray(np.asarray(L0_attn_v_weight, dtype=np.float32)),
        wg=np.ascontiguousarray(np.asarray(L0_attn_Wg, dtype=np.float32)),
        wo=np.ascontiguousarray(np.asarray(L0_attn_Wout, dtype=np.float32)),
        gp=np.ascontiguousarray(np.asarray(L0_ffn_gate_W_packed, dtype=np.int32)),
        gs=np.ascontiguousarray(np.asarray(L0_ffn_gate_scales, dtype=np.float32)),
        up=np.ascontiguousarray(np.asarray(L0_ffn_up_W_packed, dtype=np.int32)),
        us=np.ascontiguousarray(np.asarray(L0_ffn_up_scales, dtype=np.float32)),
        dp=np.ascontiguousarray(np.asarray(L0_ffn_down_W_packed, dtype=np.int32)),
        ds=np.ascontiguousarray(np.asarray(L0_ffn_down_scales, dtype=np.float32)),
    )
    nc, es = build_nc(debug=False)
    try:
        in_maps = [dict(weights, x=np.ascontiguousarray(x[b])) for b in range(8)]
        res = run_bass_kernel_spmd(nc, in_maps, core_ids=list(range(8)))
    finally:
        es.close()
    return np.stack([r["out"] for r in res.results], axis=0)


if __name__ == "__main__":
    rng = np.random.RandomState(0)
    print("smoke build only")
    nc, es = build_nc()
    es.close()
    print("built ok")


# version 12 (final)
# speedup vs baseline: 1.5528x; 1.2247x over previous
import contextlib
import numpy as np

import concourse.bass as bass
import concourse.mybir as mybir
from concourse.bass_utils import run_bass_kernel_spmd

F32 = mybir.dt.float32
BF16 = mybir.dt.bfloat16
I32 = mybir.dt.int32
I16 = mybir.dt.int16
U8 = mybir.dt.uint8
AF = mybir.ActivationFunctionType
ALU = mybir.AluOpType

P = 128
D = 2048
NTOK = 4096
H = 16
DK = 128
M = 4096
MLP = 16384
TA = 512
NT_A = NTOK // TA
TB = 512
NT_B = NTOK // TB
EC = D // P
SQD = float(DK) ** 0.5
GELU_A = 0.044715
GELU_S = 2.0 * 0.7978845608028654

ENGINES = ("sync", "scalar", "vector", "tensor", "gpsimd")


class Buf:
    __slots__ = ("w", "r")

    def __init__(self, init=None):
        self.w = dict(init) if init else {}
        self.r = dict(init) if init else {}


def _merge(dst, src):
    for kk, v in src.items():
        if dst.get(kk, -1) < v:
            dst[kk] = v


def buf_alias(*srcs):
    b = Buf()
    for s in srcs:
        _merge(b.w, s.w)
        _merge(b.w, s.r)
    return b


class K:

    def __init__(self, nc):
        self.nc = nc
        self.ops = {e: [] for e in ENGINES}
        self.cnt = {e: 0 for e in ENGINES}
        self.qcnt = {}

    def _deps(self, reads, writes, extra):
        deps = {}
        for b in reads:
            _merge(deps, b.w)
        for b in writes:
            _merge(deps, b.w)
            _merge(deps, b.r)
        for tok in extra:
            if tok is not None:
                _merge(deps, {tok[0]: tok[1]})
        return deps

    def _record(self, tok, reads, writes):
        d = {tok[0]: tok[1]}
        for b in reads:
            _merge(b.r, d)
        for b in writes:
            _merge(b.w, d)

    def op(self, eng, fn, reads=(), writes=(), extra=()):
        deps = self._deps(reads, writes, extra)
        self.cnt[eng] += 1
        tok = (("e", eng), self.cnt[eng])
        self.ops[eng].append((fn, deps, ("e", eng), 1))
        self._record(tok, reads, writes)
        return tok

    def dma(self, eng, q, fn, reads=(), writes=(), extra=()):
        deps = self._deps(reads, writes, extra)
        self.qcnt[q] = self.qcnt.get(q, 0) + 16
        tok = (("q", q), self.qcnt[q])
        self.ops[eng].append((fn, deps, ("q", q), 16))
        self._record(tok, reads, writes)
        return tok

    def barrier_toks(self):
        toks = {("e", e): self.cnt[e] for e in ENGINES if self.cnt[e] > 0}
        toks.update({("q", q): v for q, v in self.qcnt.items()})
        return toks

    def emit(self, es):
        sems = {}
        for e in ENGINES:
            if self.cnt[e] > 0:
                sems[("e", e)] = es.enter_context(self.nc.semaphore(f"se_{e}"))
        for q in self.qcnt:
            sems[("q", q)] = es.enter_context(self.nc.semaphore(f"sq_{q}"))
        block = es.enter_context(self.nc.Block())
        final_cnt = dict(self.cnt)
        for e in ENGINES:
            if not self.ops[e]:
                continue

            def make(e):
                oplist = self.ops[e]

                def body(eng):
                    waited = {}
                    for fn, deps, skey, inc in oplist:
                        for key in sorted(deps, key=repr):
                            val = deps[key]
                            if key == ("e", e):
                                if e == "tensor":
                                    continue
                                if val >= final_cnt[e]:
                                    continue
                            if waited.get(key, 0) >= val:
                                continue
                            eng.wait_ge(sems[key], val)
                            waited[key] = val
                        fn(eng).then_inc(sems[skey], inc)
                return body

            getattr(block, e)(make(e))


def build_nc(debug=False):
    nc = bass.Bass("TRN2", target_bir_lowering=False, debug=False)

    x_h = nc.dram_tensor("x", [NTOK, D], F32, kind="ExternalInput").ap()
    wq_h = nc.dram_tensor("wq", [D, D], F32, kind="ExternalInput").ap()
    wk_h = nc.dram_tensor("wk", [M, D], F32, kind="ExternalInput").ap()
    wv_h = nc.dram_tensor("wv", [M, D], F32, kind="ExternalInput").ap()
    wg_h = nc.dram_tensor("wg", [D, D], F32, kind="ExternalInput").ap()
    wo_h = nc.dram_tensor("wo", [D, D], F32, kind="ExternalInput").ap()
    gp_h = nc.dram_tensor("gp", [MLP, D // 2], I32, kind="ExternalInput").ap()
    gs_h = nc.dram_tensor("gs", [MLP, D // 32], F32, kind="ExternalInput").ap()
    up_h = nc.dram_tensor("up", [MLP, D // 2], I32, kind="ExternalInput").ap()
    us_h = nc.dram_tensor("us", [MLP, D // 32], F32, kind="ExternalInput").ap()
    dp_h = nc.dram_tensor("dp", [D, MLP // 2], I32, kind="ExternalInput").ap()
    ds_h = nc.dram_tensor("ds", [D, MLP // 32], F32, kind="ExternalInput").ap()
    out_h = nc.dram_tensor("out", [NTOK, D], F32, kind="ExternalOutput").ap()

    ikind = "ExternalOutput" if debug else "Internal"
    wq16 = nc.dram_tensor("wq16", [D, D], BF16, kind="Internal").ap()
    wg16 = nc.dram_tensor("wg16", [D, D], BF16, kind="Internal").ap()
    wo16 = nc.dram_tensor("wo16", [D, D], BF16, kind="Internal").ap()
    k16 = nc.dram_tensor("k16", [M, D], BF16, kind="Internal").ap()
    wqT_t = nc.dram_tensor("wqT_t", [EC, P, EC, P], BF16, kind="Internal").ap()
    wgT_t = nc.dram_tensor("wgT_t", [EC, P, EC, P], BF16, kind="Internal").ap()
    woT_t = nc.dram_tensor("woT_t", [4, 2, P, 8, 512], BF16, kind="Internal").ap()
    kT = nc.dram_tensor("kT", [D, M], BF16, kind="Internal").ap()
    v16 = nc.dram_tensor("v16", [P, H, 32, DK], BF16, kind="Internal").ap()
    natg = nc.dram_tensor("natg", [MLP, D], BF16, kind=ikind).ap()
    natu = nc.dram_tensor("natu", [MLP, D], BF16, kind="Internal").ap()
    natd = nc.dram_tensor("natd", [D, MLP], BF16, kind="Internal").ap()
    wgT_f = nc.dram_tensor("wgT_f", [MLP // P, P, EC, P], BF16, kind=ikind).ap()
    wuT_f = nc.dram_tensor("wuT_f", [MLP // P, P, EC, P], BF16, kind="Internal").ap()
    wdT = nc.dram_tensor("wdT", [MLP, D], BF16, kind=ikind).ap()
    xls = nc.dram_tensor("xls", [NTOK, D], BF16, kind=ikind).ap()
    yls = nc.dram_tensor("yls", [NTOK, D], BF16, kind=ikind).ap()
    ys = nc.dram_tensor("ys", [NTOK, D], F32, kind=ikind).ap()

    k = K(nc)
    es = contextlib.ExitStack()
    sbuf = nc.sbuf_tensor
    psum = nc.psum_tensor

    stt = [es.enter_context(sbuf(f"stt{i}", [P, 32], F32)) for i in range(2)]
    b_stt = [Buf() for _ in range(2)]
    stg16 = [es.enter_context(sbuf("stg16_0", [P, 4096], BF16)),
             es.enter_context(sbuf("stg16_1", [P, 2048], BF16))]
    b_stg16 = [Buf() for _ in range(2)]
    xstg = [es.enter_context(sbuf(f"xstg{i}", [P, D], F32)) for i in range(2)]
    b_xstg = [Buf() for _ in range(2)]
    lstg = [es.enter_context(sbuf(f"lstg{i}", [P, D], BF16)) for i in range(2)]
    b_lstg = [Buf() for _ in range(2)]
    xln = es.enter_context(sbuf("xlnf", [P, EC * TA], BF16))
    qf = es.enter_context(sbuf("qf", [P, H * TA], BF16))
    gf = es.enter_context(sbuf("gf", [P, H * TA], BF16))
    ogf = es.enter_context(sbuf("ogf", [P, H * TA], BF16))
    b_xln = [Buf() for _ in range(EC)]
    b_qf, b_gf, b_ogf = Buf(), Buf(), Buf()
    wsl = [es.enter_context(sbuf(f"wsl{i}", [P, EC * P], BF16)) for i in range(2)]
    b_wsl = [Buf() for _ in range(2)]
    wosl = [es.enter_context(sbuf(f"wosl{i}", [P, 8 * 512], BF16)) for i in range(2)]
    b_wosl = [Buf() for _ in range(2)]
    ktsl = [es.enter_context(sbuf(f"ktsl{i}", [P, M], BF16)) for i in range(2)]
    b_ktsl = [Buf() for _ in range(2)]
    vsl = [es.enter_context(sbuf(f"vsl{i}", [P, 32 * DK], BF16)) for i in range(2)]
    b_vsl = [Buf() for _ in range(2)]
    ensb = [es.enter_context(sbuf(f"ensb{i}", [P, TA], BF16)) for i in range(6)]
    b_ensb = [Buf() for _ in range(6)]
    ypc = [es.enter_context(sbuf(f"ypc{i}", [P, 512], F32)) for i in range(4)]
    b_ypc = [Buf() for _ in range(4)]
    xpc = [es.enter_context(sbuf(f"xpc{i}", [P, 512], F32)) for i in range(2)]
    b_xpc = [Buf() for _ in range(2)]
    dvd_sh = es.enter_context(sbuf("dvd", [P, 2048], BF16))
    b_dvd_sh = Buf()
    dec = []
    for i in range(2):
        dec.append(dict(
            u8=es.enter_context(sbuf(f"du8_{i}", [P, 1024], U8)),
            m=es.enter_context(sbuf(f"dm_{i}", [P, 2048], I16)),
            s=es.enter_context(sbuf(f"dsx_{i}", [P, 2048], I16)),
            c=es.enter_context(sbuf(f"dcx_{i}", [P, 2048], I16)),
            vd=dvd_sh, sc=es.enter_context(sbuf(f"dsc_{i}", [P, 64], F32)),
            sc16=es.enter_context(sbuf(f"dsc16_{i}", [P, 64], BF16)),
            b_u8=Buf(), b_m=Buf(), b_s=Buf(), b_c=Buf(), b_vd=b_dvd_sh,
            b_sc=Buf(), b_sc16=Buf(),
        ))

    p_en = [es.enter_context(psum(f"p_en{i}", [P, 512], F32)) for i in range(2)]
    b_pen = [Buf() for _ in range(2)]
    p_out = [es.enter_context(psum(f"p_out{i}", [P, 512], F32)) for i in range(2)]
    b_pout = [Buf() for _ in range(2)]
    p_mm = [es.enter_context(psum(f"p_mm{i}", [P, 512], F32)) for i in range(4)]
    b_pmm = [Buf() for _ in range(4)]

    b_wq16, b_wg16, b_wo16, b_k16 = Buf(), Buf(), Buf(), Buf()
    b_wqT, b_wgT, b_woT, b_kT, b_v16 = Buf(), Buf(), Buf(), Buf(), Buf()
    b_natg = [Buf() for _ in range(MLP // P)]
    b_natu = [Buf() for _ in range(MLP // P)]
    b_natd = [Buf() for _ in range(128)]
    b_wgTf = [Buf() for _ in range(8)]
    b_wuTf = [Buf() for _ in range(8)]
    b_wdTg = [Buf() for _ in range(8)]
    b_xls = [[Buf() for _ in range(4)] for _ in range(NT_A)]
    b_yls = [[Buf() for _ in range(4)] for _ in range(NT_A)]
    b_ys = [[Buf() for _ in range(4)] for _ in range(32)]

    def cast_to_16(src, dst, nrow, b_dst, tag):
        for c0 in range(nrow // P):
            sl = c0 % 2
            k.dma("gpsimd", f"stg{sl}",
                  lambda g, sl=sl, c0=c0, src=src:
                  g.dma_start(out=stg16[sl][:, 0:D],
                              in_=src[c0 * P:(c0 + 1) * P, :]),
                  writes=(b_stg16[sl],))
            k.dma("sync", f"stgw{sl}",
                  lambda s, sl=sl, c0=c0, dst=dst:
                  s.dma_start(out=dst[c0 * P:(c0 + 1) * P, :],
                              in_=stg16[sl][:, 0:D]),
                  reads=(b_stg16[sl],), writes=(b_dst,))

    cast_to_16(wq_h, wq16, D, b_wq16, "wq")
    cast_to_16(wg_h, wg16, D, b_wg16, "wg")
    cast_to_16(wo_h, wo16, D, b_wo16, "wo")
    cast_to_16(wk_h, k16, M, b_k16, "wk")

    for c in range(32):
        sl = c % 2
        k.dma("gpsimd", f"vsl{sl}",
              lambda g, c=c, sl=sl:
              g.dma_start(out=vsl[sl][:, 0:D], in_=wv_h[c * P:(c + 1) * P, :]),
              writes=(b_vsl[sl],))
        k.dma("sync", f"v16w{sl}",
              lambda s, c=c, sl=sl:
              s.dma_start(out=v16[:, :, c, :],
                          in_=vsl[sl][:, 0:D].rearrange("p (h vd) -> p h vd", vd=DK)),
              reads=(b_vsl[sl],), writes=(b_v16,))

    def xbar_tiled(src16, b_src, dstT, b_dst):
        for ec in range(EC):
            sl = ec % 2
            k.dma("sync", f"stg{sl}",
                  lambda s, ec=ec, sl=sl, src16=src16:
                  s.dma_start_transpose(stg16[sl][:, 0:D], src16[:, ec * P:(ec + 1) * P]),
                  reads=(b_src,), writes=(b_stg16[sl],))
            for half in range(2):
                k.dma("sync", f"stgw{sl}",
                      lambda s, ec=ec, sl=sl, half=half, dstT=dstT:
                      s.dma_start(out=dstT[half * 8:(half + 1) * 8, :, ec].transpose([1, 0, 2]),
                                  in_=stg16[sl][:, half * 1024:(half + 1) * 1024]
                                  .rearrange("p (fc col) -> p fc col", col=P)),
                      reads=(b_stg16[sl],), writes=(b_dst,))

    xbar_tiled(wq16, b_wq16, wqT_t, b_wqT)
    xbar_tiled(wg16, b_wg16, wgT_t, b_wgT)

    for vc in range(EC):
        sl = vc % 2
        k.dma("sync", f"stg{sl}",
              lambda s, vc=vc, sl=sl:
              s.dma_start_transpose(stg16[sl][:, 0:D], wo16[:, vc * P:(vc + 1) * P]),
              reads=(b_wo16,), writes=(b_stg16[sl],))
        for eq in range(4):
            k.dma("sync", f"stgw{sl}",
                  lambda s, vc=vc, sl=sl, eq=eq:
                  s.dma_start(out=woT_t[eq, vc // 8, :, vc % 8],
                              in_=stg16[sl][:, eq * 512:(eq + 1) * 512]),
                  reads=(b_stg16[sl],), writes=(b_woT,))

    for c in range(EC):
        sl = 0
        k.dma("sync", f"stg{sl}",
              lambda s, c=c, sl=sl:
              s.dma_start_transpose(stg16[sl][:, 0:M], k16[:, c * P:(c + 1) * P]),
              reads=(b_k16,), writes=(b_stg16[sl],))
        k.dma("sync", f"stgw{sl}",
              lambda s, c=c, sl=sl:
              s.dma_start(out=kT.rearrange("(c p) m -> c p m", p=P)[c],
                          in_=stg16[sl][:, 0:M]),
              reads=(b_stg16[sl],), writes=(b_kT,))

    def decode_subtile(tail_eng, pack_ap, sc_ap, nat_out_ap, b_nat, slot):
        d = dec[slot]
        k.dma("gpsimd", f"du8{slot}",
              lambda g, pack_ap=pack_ap, d=d:
              g.dma_start(out=d["u8"][:, :], in_=pack_ap),
              writes=(d["b_u8"],))
        k.dma("gpsimd", f"dsc{slot}",
              lambda g, sc_ap=sc_ap, d=d:
              g.dma_start(out=d["sc"][:, :], in_=sc_ap),
              writes=(d["b_sc"],))
        w16 = d["u8"].ap().bitcast(I16)
        for j in range(4):
            if j == 0:
                k.op("vector", lambda e, d=d, w16=w16:
                     e.tensor_single_scalar(out=d["m"][:, 0:512], in_=w16, scalar=7,
                                            op=ALU.bitwise_and),
                     reads=(d["b_u8"],), writes=(d["b_m"],))
            else:
                k.op("vector", lambda e, d=d, w16=w16, j=j:
                     e.tensor_scalar(out=d["m"][:, j * 512:(j + 1) * 512], in0=w16,
                                     scalar1=4 * j, scalar2=7,
                                     op0=ALU.logical_shift_right, op1=ALU.bitwise_and),
                     reads=(d["b_u8"],), writes=(d["b_m"],))
        for j in range(4):
            k.op("vector", lambda e, d=d, w16=w16, j=j:
                 e.tensor_scalar(out=d["s"][:, j * 512:(j + 1) * 512], in0=w16,
                                 scalar1=8 << (4 * j), scalar2=12 - 4 * j,
                                 op0=ALU.bitwise_and, op1=ALU.logical_shift_left),
                 reads=(d["b_u8"],), writes=(d["b_s"],))
        k.op("vector", lambda e, d=d:
             e.tensor_single_scalar(out=d["c"][:, :], in_=d["m"][:, :], scalar=2, op=ALU.min),
             reads=(d["b_m"],), writes=(d["b_c"],))
        k.op("vector", lambda e, d=d:
             e.tensor_scalar(out=d["m"][:, :], in0=d["m"][:, :], scalar1=252, scalar2=254,
                             op0=ALU.add, op1=ALU.max),
             reads=(d["b_m"],), writes=(d["b_m"],))
        k.op("vector", lambda e, d=d:
             e.tensor_single_scalar(out=d["m"][:, :], in_=d["m"][:, :], scalar=6,
                                    op=ALU.logical_shift_left),
             reads=(d["b_m"],), writes=(d["b_m"],))
        k.op("vector", lambda e, d=d:
             e.tensor_tensor(out=d["m"][:, :], in0=d["m"][:, :], in1=d["s"][:, :],
                             op=ALU.add),
             reads=(d["b_m"], d["b_s"]), writes=(d["b_m"],))
        corrf = d["s"].ap().bitcast(BF16)
        k.op(tail_eng, lambda e, d=d, corrf=corrf:
             e.tensor_single_scalar(out=corrf, in_=d["c"][:, :], scalar=0.5, op=ALU.mult),
             reads=(d["b_c"],), writes=(d["b_s"],))
        val = d["c"].ap().bitcast(BF16)
        k.op(tail_eng, lambda e, d=d, corrf=corrf, val=val:
             e.tensor_mul(out=val, in0=d["m"].ap().bitcast(BF16), in1=corrf),
             reads=(d["b_m"], d["b_s"]), writes=(d["b_c"],))
        k.op(tail_eng, lambda e, d=d:
             e.tensor_copy(out=d["sc16"][:, :], in_=d["sc"][:, :]),
             reads=(d["b_sc"],), writes=(d["b_sc16"],))
        vdr = d["vd"].ap().rearrange("p (g four) -> p four g", four=4)
        for j in range(4):
            k.op(tail_eng, lambda e, d=d, val=val, vdr=vdr, j=j:
                 e.tensor_tensor(
                     out=vdr[:, j, :].rearrange("p (gb gg) -> p gb gg", gg=8),
                     in0=val[:, j * 512:(j + 1) * 512].rearrange("p (gb gg) -> p gb gg", gg=8),
                     in1=d["sc16"].ap().unsqueeze(2).to_broadcast([P, 64, 8]),
                     op=ALU.mult),
                 reads=(d["b_c"], d["b_sc16"]), writes=(d["b_vd"],))
        k.dma("gpsimd", f"dnw{slot}",
              lambda g, nat_out_ap=nat_out_ap, d=d:
              g.dma_start(out=nat_out_ap, in_=d["vd"][:, :]),
              reads=(d["b_vd"],), writes=(b_nat,))

    decode_jobs = []
    for j in range(MLP // P):
        r = j
        decode_jobs.append((gp_h[r * P:(r + 1) * P, :], gs_h[r * P:(r + 1) * P, :],
                            natg[r * P:(r + 1) * P, :], b_natg[r]))
        decode_jobs.append((up_h[r * P:(r + 1) * P, :], us_h[r * P:(r + 1) * P, :],
                            natu[r * P:(r + 1) * P, :], b_natu[r]))
        s8, r2 = j // 16, j % 16
        decode_jobs.append((dp_h[r2 * P:(r2 + 1) * P, s8 * 1024:(s8 + 1) * 1024],
                            ds_h[r2 * P:(r2 + 1) * P, s8 * 64:(s8 + 1) * 64],
                            natd[r2 * P:(r2 + 1) * P, s8 * 2048:(s8 + 1) * 2048],
                            b_natd[r2 * 8 + s8]))
    n_jobs = len(decode_jobs)
    dec_state = {"i": 0}

    def emit_decode(n):
        stop = min(dec_state["i"] + n, n_jobs)
        while dec_state["i"] < stop:
            i = dec_state["i"]
            pa, sa, na, bn = decode_jobs[i]
            decode_subtile("vector", pa, sa, na, bn, i % 2)
            dec_state["i"] += 1

    def ffn_transpose_tasks(idx):
        G = idx
        tasks = []
        for nat, b_nat, dstT, b_dst in ((natg, b_natg, wgT_f, b_wgTf[G]),
                                        (natu, b_natu, wuT_f, b_wuTf[G])):
            rdeps = tuple(b_nat[G * 16 + i] for i in range(16))
            for ec in range(EC):
                sl = ec % 2

                def go(nat=nat, dstT=dstT, b_dst=b_dst, rdeps=rdeps, ec=ec, sl=sl, G=G):
                    k.dma("sync", f"stg{sl}",
                          lambda g, nat=nat, G=G, ec=ec, sl=sl:
                          g.dma_start_transpose(
                              stg16[sl][:, 0:2048],
                              nat[G * 2048:(G + 1) * 2048, ec * P:(ec + 1) * P]),
                          reads=rdeps, writes=(b_stg16[sl],))
                    for half in range(2):
                        k.dma("sync", f"stgw{sl}",
                              lambda g, dstT=dstT, G=G, ec=ec, sl=sl, half=half:
                              g.dma_start(
                                  out=dstT[G * 16 + half * 8:G * 16 + (half + 1) * 8, :, ec]
                                  .transpose([1, 0, 2]),
                                  in_=stg16[sl][:, half * 1024:(half + 1) * 1024]
                                  .rearrange("p (mc col) -> p mc col", col=P)),
                              reads=(b_stg16[sl],), writes=(b_dst,))
                tasks.append(go)
        for mc in range(idx * 16, (idx + 1) * 16):
            sl = mc % 2
            rdeps = tuple(b_natd[r * 8 + (mc // 16)] for r in range(16))

            def go(mc=mc, sl=sl, rdeps=rdeps, G=G):
                k.dma("sync", f"stg{sl}",
                      lambda g, mc=mc, sl=sl:
                      g.dma_start_transpose(stg16[sl][:, 0:D], natd[:, mc * P:(mc + 1) * P]),
                      reads=rdeps, writes=(b_stg16[sl],))
                k.dma("sync", f"stgw{sl}",
                      lambda g, mc=mc, sl=sl:
                      g.dma_start(out=wdT.rearrange("(mc p) h -> mc p h", p=P)[mc],
                                  in_=stg16[sl][:, 0:D]),
                      reads=(b_stg16[sl],), writes=(b_wdTg[G],))
            tasks.append(go)
        return tasks

    def ln_chunk(xb, b_xb, outb, b_outb, si):
        st = stt[si]
        b_st = b_stt[si]
        for g4 in range(4):
            k.op("vector", lambda v, g4=g4, xb=xb, st=st:
                 v.bn_stats(out=st[:, g4 * 6:(g4 + 1) * 6],
                            in_=xb[:, g4 * 512:(g4 + 1) * 512]),
                 reads=(b_xb,), writes=(b_st,))
        k.op("vector", lambda v, st=st: v.bn_aggr(out=st[:, 24:26], in_=st[:, 0:24]),
             reads=(b_st,), writes=(b_st,))
        k.op("vector", lambda v, st=st:
             v.tensor_scalar_add(out=st[:, 25:26], in0=st[:, 25:26], scalar1=1e-5),
             reads=(b_st,), writes=(b_st,))
        k.op("scalar", lambda s, st=st:
             s.activation(out=st[:, 31:32], in_=st[:, 25:26], func=AF.Sqrt),
             reads=(b_st,), writes=(b_st,))
        k.op("vector", lambda v, st=st: v.reciprocal(out=st[:, 26:27], in_=st[:, 31:32]),
             reads=(b_st,), writes=(b_st,))
        k.op("vector", lambda v, st=st:
             v.tensor_mul(out=st[:, 27:28], in0=st[:, 25:26], in1=st[:, 26:27]),
             reads=(b_st,), writes=(b_st,))
        k.op("vector", lambda v, st=st:
             v.tensor_mul(out=st[:, 28:29], in0=st[:, 27:28], in1=st[:, 26:27]),
             reads=(b_st,), writes=(b_st,))
        k.op("vector", lambda v, st=st:
             v.tensor_scalar(out=st[:, 28:29], in0=st[:, 28:29], scalar1=-0.5, scalar2=1.5,
                             op0=ALU.mult, op1=ALU.add),
             reads=(b_st,), writes=(b_st,))
        k.op("vector", lambda v, st=st:
             v.tensor_mul(out=st[:, 29:30], in0=st[:, 26:27], in1=st[:, 28:29]),
             reads=(b_st,), writes=(b_st,))
        k.op("vector", lambda v, st=st:
             v.scalar_tensor_tensor(out=st[:, 30:31], in0=st[:, 24:25], scalar=-1.0,
                                    in1=st[:, 29:30], op0=ALU.mult, op1=ALU.mult),
             reads=(b_st,), writes=(b_st,))
        k.op("scalar", lambda s, xb=xb, outb=outb, st=st:
             s.activation(out=outb, in_=xb, func=AF.Identity,
                          scale=st[:, 29:30], bias=st[:, 30:31]),
             reads=(b_xb, b_st), writes=(b_outb,))

    def ln1_task(t, tc):
        t0 = t * TA
        sl = tc % 2

        def go():
            k.dma("sync", f"xst{sl}",
                  lambda s, t0=t0, tc=tc, sl=sl:
                  s.dma_start(out=xstg[sl][:, :], in_=x_h[t0 + tc * P:t0 + (tc + 1) * P, :]),
                  writes=(b_xstg[sl],))
            ln_chunk(xstg[sl][:, :], b_xstg[sl], lstg[sl][:, :], b_lstg[sl], sl)
            k.dma("sync", f"lst{sl}",
                  lambda s, t0=t0, tc=tc, sl=sl:
                  s.dma_start(out=xls[t0 + tc * P:t0 + (tc + 1) * P, :], in_=lstg[sl][:, :]),
                  reads=(b_lstg[sl],), writes=(b_xls[t][tc],))
        return go

    def ln2_task(t, tc):
        t0 = t * TA
        sl = tc % 2

        def go():
            k.dma("sync", f"xst{sl}",
                  lambda s, t0=t0, tc=tc, sl=sl:
                  s.dma_start(out=xstg[sl][:, :], in_=ys[t0 + tc * P:t0 + (tc + 1) * P, :]),
                  reads=tuple(b_ys[t * 4 + tc]), writes=(b_xstg[sl],))
            ln_chunk(xstg[sl][:, :], b_xstg[sl], lstg[sl][:, :], b_lstg[sl], sl)
            k.dma("sync", f"lst{sl}",
                  lambda s, t0=t0, tc=tc, sl=sl:
                  s.dma_start(out=yls[t0 + tc * P:t0 + (tc + 1) * P, :], in_=lstg[sl][:, :]),
                  reads=(b_lstg[sl],), writes=(b_yls[t][tc],))
        return go

    def emit_xbar_a(t):
        t0 = t * TA
        xt_last = None
        for ec in range(EC):
            xt_last = k.dma("sync", "xbar",
                            lambda s, t0=t0, ec=ec:
                            s.dma_start_transpose(xln[:, ec * TA:(ec + 1) * TA],
                                                  xls[t0:t0 + TA, ec * P:(ec + 1) * P]),
                            reads=tuple(b_xls[t]), writes=(b_xln[ec],))
        for ec in range(EC):
            _merge(b_xln[ec].w, {xt_last[0]: xt_last[1]})

    def emit_head_loads(h):
        sl = h % 2
        k.dma("sync", f"kts{sl}",
              lambda s, h=h, sl=sl:
              s.dma_start(out=ktsl[sl][:, :],
                          in_=kT.rearrange("(c p) m -> c p m", p=P)[h]),
              reads=(b_kT,), writes=(b_ktsl[sl],))
        k.dma("gpsimd", f"vsl{sl}",
              lambda g, h=h, sl=sl:
              g.dma_start(out=vsl[sl][:, :].rearrange("p (c dd) -> p c dd", dd=DK),
                          in_=v16[:, h]),
              reads=(b_v16,), writes=(b_vsl[sl],))

    def phase_a_tile(t, tasks):
        t0 = t * TA
        emit_head_loads(0)
        for wT, b_wT, dstbuf, b_dst, act in ((wqT_t, b_wqT, qf, b_qf, None),
                                             (wgT_t, b_wgT, gf, b_gf, AF.Silu)):
            for fc in range(EC):
                sl = fc % 2
                k.dma("sync", f"wsl{sl}",
                      lambda sy, wT=wT, fc=fc, sl=sl:
                      sy.dma_start(out=wsl[sl][:, :].rearrange("p (ec col) -> p ec col", col=P),
                                  in_=wT[fc]),
                      reads=(b_wT,), writes=(b_wsl[sl],))
                pm = fc % 4
                for ecc in range(EC):
                    k.op("tensor", lambda te, sl=sl, ecc=ecc, pm=pm:
                         te.matmul(out=p_mm[pm][:, :],
                                   lhsT=wsl[sl][:, ecc * P:(ecc + 1) * P],
                                   rhs=xln[:, ecc * TA:ecc * TA + TA],
                                   start=(ecc == 0), stop=(ecc == EC - 1)),
                         reads=(b_wsl[sl], b_xln[ecc]), writes=(b_pmm[pm],))
                if act is None:
                    k.op("vector", lambda v, fc=fc, pm=pm, dstbuf=dstbuf:
                         v.tensor_copy(out=dstbuf[:, fc * TA:(fc + 1) * TA],
                                       in_=p_mm[pm][:, :]),
                         reads=(b_pmm[pm],), writes=(b_dst,))
                else:
                    k.op("scalar", lambda s, fc=fc, pm=pm, dstbuf=dstbuf, act=act:
                         s.activation(out=dstbuf[:, fc * TA:(fc + 1) * TA],
                                      in_=p_mm[pm][:, :], func=act),
                         reads=(b_pmm[pm],), writes=(b_dst,))
        for h in range(H):
            sl = h % 2
            if h + 1 < H:
                emit_head_loads(h + 1)
            po = h % 2
            en_pool = [p_en[0], p_en[1], p_mm[0], p_mm[1], p_mm[2], p_mm[3]]
            b_en_pool = [b_pen[0], b_pen[1], b_pmm[0], b_pmm[1], b_pmm[2], b_pmm[3]]
            for cc in range(36):
                if cc < 32:
                    c, pes = cc, cc % 6
                    k.op("tensor", lambda te, sl=sl, c=c, pes=pes, h=h:
                         te.matmul(out=en_pool[pes][:, :],
                                   lhsT=ktsl[sl][:, c * P:(c + 1) * P],
                                   rhs=qf[:, h * TA:(h + 1) * TA],
                                   start=True, stop=True),
                         reads=(b_ktsl[sl], b_qf), writes=(b_en_pool[pes],))
                    k.op("scalar", lambda s, pes=pes:
                         s.activation(out=ensb[pes][:, :], in_=en_pool[pes][:, :],
                                      func=AF.Silu, scale=SQD),
                         reads=(b_en_pool[pes],), writes=(b_ensb[pes],))
                if cc >= 4:
                    c, pes = cc - 4, (cc - 4) % 6
                    k.op("tensor", lambda te, sl=sl, c=c, pes=pes, po=po:
                         te.matmul(out=p_out[po][:, :],
                                   lhsT=vsl[sl][:, c * DK:(c + 1) * DK],
                                   rhs=ensb[pes][:, :],
                                   start=(c == 0), stop=(c == 31)),
                         reads=(b_vsl[sl], b_ensb[pes]), writes=(b_pout[po],))
            k.op("vector", lambda v, h=h, po=po:
                 v.scalar_tensor_tensor(out=ogf[:, h * TA:(h + 1) * TA],
                                        in0=p_out[po][:, :], scalar=SQD,
                                        in1=gf[:, h * TA:(h + 1) * TA],
                                        op0=ALU.mult, op1=ALU.mult),
                 reads=(b_pout[po], b_gf), writes=(b_ogf,))
            emit_decode(3)
            for _ in range(4):
                if tasks:
                    tasks.pop(0)()
        for eq in range(4):
            for vh in range(2):
                sl = (eq * 2 + vh) % 2
                k.dma("sync", f"wos{sl}",
                      lambda sy, eq=eq, vh=vh, sl=sl:
                      sy.dma_start(out=wosl[sl][:, :]
                                  .rearrange("p (vc col) -> p vc col", col=512),
                                  in_=woT_t[eq, vh]),
                      reads=(b_woT,), writes=(b_wosl[sl],))
                for ts in range(4):
                    for vc8 in range(8):
                        vc = vh * 8 + vc8
                        k.op("tensor",
                             lambda te, ts=ts, vc=vc, vc8=vc8, sl=sl, vh=vh:
                             te.matmul(out=p_mm[ts][:, :],
                                       lhsT=ogf[:, vc * TA + ts * P:vc * TA + (ts + 1) * P],
                                       rhs=wosl[sl][:, vc8 * 512:(vc8 + 1) * 512],
                                       start=(vh == 0 and vc8 == 0),
                                       stop=(vh == 1 and vc8 == 7)),
                             reads=(b_ogf, b_wosl[sl]), writes=(b_pmm[ts],))
            for ts in range(4):
                xsl = ts % 2
                k.dma("sync", f"xpc{xsl}",
                      lambda s, t0=t0, eq=eq, ts=ts, xsl=xsl:
                      s.dma_start(out=xpc[xsl][:, :],
                                  in_=x_h[t0 + ts * P:t0 + (ts + 1) * P,
                                          eq * 512:(eq + 1) * 512]),
                      writes=(b_xpc[xsl],))
                k.op("vector", lambda v, ts=ts, xsl=xsl:
                     v.tensor_add(out=ypc[ts][:, :], in0=p_mm[ts][:, :], in1=xpc[xsl][:, :]),
                     reads=(b_pmm[ts], b_xpc[xsl]), writes=(b_ypc[ts],))
                k.dma("sync", f"ypc{ts}",
                      lambda s, t0=t0, eq=eq, ts=ts:
                      s.dma_start(out=ys[t0 + ts * P:t0 + (ts + 1) * P,
                                         eq * 512:(eq + 1) * 512],
                                  in_=ypc[ts][:, :]),
                      reads=(b_ypc[ts],), writes=(b_ys[t * 4 + ts][eq],))
        while tasks:
            tasks.pop(0)()

    for tc in range(4):
        ln1_task(0, tc)()
    emit_xbar_a(0)
    for t in range(NT_A):
        tasks = []
        if t >= 1:
            tasks += [ln2_task(t - 1, tc) for tc in range(4)]
        if t + 1 < NT_A:
            tasks += [ln1_task(t + 1, tc) for tc in range(4)]
        if t >= 1:
            tr = ffn_transpose_tasks(t - 1)
            merged = []
            li = 0
            while tr or li < len(tasks):
                for _ in range(3):
                    if tr:
                        merged.append(tr.pop(0))
                if li < len(tasks):
                    merged.append(tasks[li])
                    li += 1
            tasks = merged
        phase_a_tile(t, tasks)
        if t + 1 < NT_A:
            emit_xbar_a(t + 1)
    for tc in range(4):
        ln2_task(7, tc)()
    emit_decode(n_jobs)
    for go in ffn_transpose_tasks(7):
        go()

    yln = xln
    b_yln = [buf_alias(b_xln[ec]) for ec in range(EC)]
    inter = qf
    b_inter = buf_alias(b_qf)
    accA = gf.ap().bitcast(F32)
    accB = ogf.ap().bitcast(F32)
    acc_ap = [accA[:, 0:D], accA[:, D:2 * D], accB[:, 0:D], accB[:, D:2 * D]]
    b_acc = [buf_alias(b_gf), buf_alias(b_gf), buf_alias(b_ogf), buf_alias(b_ogf)]
    ystg = [xstg[i].ap() for i in range(2)]
    b_ystg = [buf_alias(b_xstg[i]) for i in range(2)]
    ostg = [stg16[0].ap().bitcast(F32), stg16[0].ap().bitcast(F32)]
    b_ostg0 = buf_alias(b_stg16[0], b_stg16[1])
    b_ostg = [b_ostg0, b_ostg0]
    wgsl = [ktsl[i].ap() for i in range(2)]
    b_wgsl = [buf_alias(b_ktsl[i]) for i in range(2)]
    wusl = [vsl[i].ap() for i in range(2)]
    b_wusl = [buf_alias(b_vsl[i]) for i in range(2)]
    wdsl = [wosl[0].ap(), wosl[1].ap()]
    b_wdsl = [buf_alias(b_wosl[i]) for i in range(2)]
    glt = [ypc[0], ypc[1], ypc[2], ypc[3], xpc[0]]
    b_glt = [buf_alias(b_ypc[0]), buf_alias(b_ypc[1]), buf_alias(b_ypc[2]),
             buf_alias(b_ypc[3]), buf_alias(b_xpc[0])]
    p_dn = [p_en[0].ap(), p_en[1].ap(), p_out[0].ap(), p_out[1].ap()]
    b_pdn = [buf_alias(b_pen[0]), buf_alias(b_pen[1]),
             buf_alias(b_pout[0]), buf_alias(b_pout[1])]
    b_pgu = [buf_alias(b_pmm[i]) for i in range(4)]

    def emit_xbar_b(b):
        t0 = b * TB
        yt_last = None
        for ec in range(EC):
            yt_last = k.dma("sync", "xbar",
                            lambda s, t0=t0, ec=ec:
                            s.dma_start_transpose(yln[:, ec * TB:(ec + 1) * TB],
                                                  yls[t0:t0 + TB, ec * P:(ec + 1) * P]),
                            reads=tuple(b_yls[b]), writes=(b_yln[ec],))
        for ec in range(EC):
            _merge(b_yln[ec].w, {yt_last[0]: yt_last[1]})

    def phase_b_tile(b):
        t0 = b * TB
        for G in range(8):
            for mcl in range(16):
                mc = G * 16 + mcl
                sl = (mcl // 2) % 2
                half = (mcl % 2) * 2048
                pg = mcl % 2
                pu = 2 + mcl % 2
                if mcl % 2 == 0:
                    k.dma("sync", f"wgl{sl}",
                          lambda sy, mc=mc, sl=sl:
                          sy.dma_start(
                              out=wgsl[sl].rearrange("p (two n) -> p two n", n=2048),
                              in_=wgT_f[mc:mc + 2].rearrange("mc p ec col -> mc p (ec col)")
                              .transpose([1, 0, 2])),
                          reads=(b_wgTf[G],), writes=(b_wgsl[sl],))
                    k.dma("sync", f"wul{sl}",
                          lambda sy, mc=mc, sl=sl:
                          sy.dma_start(
                              out=wusl[sl].rearrange("p (two n) -> p two n", n=2048),
                              in_=wuT_f[mc:mc + 2].rearrange("mc p ec col -> mc p (ec col)")
                              .transpose([1, 0, 2])),
                          reads=(b_wuTf[G],), writes=(b_wusl[sl],))
                for ecc in range(EC):
                    k.op("tensor", lambda te, sl=sl, ecc=ecc, pg=pg, half=half:
                         te.matmul(out=p_mm[pg][:, :],
                                   lhsT=wgsl[sl][:, half + ecc * P:half + (ecc + 1) * P],
                                   rhs=yln[:, ecc * TB:(ecc + 1) * TB],
                                   start=(ecc == 0), stop=(ecc == EC - 1)),
                         reads=(b_wgsl[sl], b_yln[ecc]), writes=(b_pgu[pg],))
                for ecc in range(EC):
                    k.op("tensor", lambda te, sl=sl, ecc=ecc, pu=pu, half=half:
                         te.matmul(out=p_mm[pu][:, :],
                                   lhsT=wusl[sl][:, half + ecc * P:half + (ecc + 1) * P],
                                   rhs=yln[:, ecc * TB:(ecc + 1) * TB],
                                   start=(ecc == 0), stop=(ecc == EC - 1)),
                         reads=(b_wusl[sl], b_yln[ecc]), writes=(b_pgu[pu],))
                k.op("scalar", lambda s, pg=pg:
                     s.activation(out=glt[0][:, :], in_=p_mm[pg][:, :], func=AF.Square),
                     reads=(b_pgu[pg],), writes=(b_glt[0],))
                k.op("vector", lambda v:
                     v.tensor_scalar(out=glt[1][:, :], in0=glt[0][:, :], scalar1=GELU_A,
                                     scalar2=1.0, op0=ALU.mult, op1=ALU.add),
                     reads=(b_glt[0],), writes=(b_glt[1],))
                k.op("vector", lambda v, pg=pg:
                     v.tensor_mul(out=glt[2][:, :], in0=p_mm[pg][:, :], in1=glt[1][:, :]),
                     reads=(b_pgu[pg], b_glt[1]), writes=(b_glt[2],))
                k.op("scalar", lambda s:
                     s.activation(out=glt[3][:, :], in_=glt[2][:, :], func=AF.Sigmoid,
                                  scale=GELU_S),
                     reads=(b_glt[2],), writes=(b_glt[3],))
                k.op("vector", lambda v, pg=pg:
                     v.tensor_mul(out=glt[4][:, :], in0=p_mm[pg][:, :], in1=glt[3][:, :]),
                     reads=(b_pgu[pg], b_glt[3]), writes=(b_glt[4],))
                k.op("vector", lambda v, pu=pu, mcl=mcl:
                     v.tensor_mul(out=inter[:, mcl * TB:(mcl + 1) * TB],
                                  in0=glt[4][:, :], in1=p_mm[pu][:, :]),
                     reads=(b_glt[4], b_pgu[pu]), writes=(b_inter,))
            if G == 7 and b + 1 < NT_B:
                emit_xbar_b(b + 1)
            for hq in range(4):
                for half in range(2):
                    k.dma("sync", f"wdl{half}",
                          lambda s, G=G, hq=hq, half=half:
                          s.dma_start(
                              out=wdsl[half].rearrange("p (mc n) -> p mc n", n=512),
                              in_=wdT[(G * 16 + half * 8) * P:(G * 16 + (half + 1) * 8) * P,
                                      hq * 512:(hq + 1) * 512]
                              .rearrange("(mc p) n -> p mc n", p=P)),
                          reads=(b_wdTg[G],), writes=(b_wdsl[half],))
                for mcl in range(16):
                    sl = mcl // 8
                    off = (mcl % 8) * 512
                    for ts in range(4):
                        k.op("tensor", lambda te, mcl=mcl, ts=ts, sl=sl, off=off:
                             te.matmul(out=p_dn[ts],
                                       lhsT=inter[:, mcl * TB + ts * P:mcl * TB + (ts + 1) * P],
                                       rhs=wdsl[sl][:, off:off + 512],
                                       start=(mcl == 0), stop=(mcl == 15)),
                             reads=(b_inter, b_wdsl[sl]), writes=(b_pdn[ts],))
                for ts in range(4):
                    if G == 0:
                        k.op("vector", lambda v, ts=ts, hq=hq:
                             v.tensor_copy(out=acc_ap[ts][:, hq * 512:(hq + 1) * 512],
                                           in_=p_dn[ts]),
                             reads=(b_pdn[ts],), writes=(b_acc[ts],))
                    else:
                        k.op("vector", lambda v, ts=ts, hq=hq:
                             v.tensor_add(out=acc_ap[ts][:, hq * 512:(hq + 1) * 512],
                                          in0=acc_ap[ts][:, hq * 512:(hq + 1) * 512],
                                          in1=p_dn[ts]),
                             reads=(b_pdn[ts], b_acc[ts]), writes=(b_acc[ts],))
        for ts in range(4):
            sl = ts % 2
            k.dma("sync", f"yst{sl}",
                  lambda s, t0=t0, ts=ts, sl=sl:
                  s.dma_start(out=ystg[sl], in_=ys[t0 + ts * P:t0 + (ts + 1) * P, :]),
                  reads=tuple(b_ys[b * 4 + ts]), writes=(b_ystg[sl],))
            k.op("vector", lambda v, ts=ts, sl=sl:
                 v.tensor_add(out=ostg[sl], in0=acc_ap[ts][:, :], in1=ystg[sl]),
                 reads=(b_acc[ts], b_ystg[sl]), writes=(b_ostg[sl],))
            k.dma("sync", f"ost{sl}",
                  lambda s, t0=t0, ts=ts, sl=sl:
                  s.dma_start(out=out_h[t0 + ts * P:t0 + (ts + 1) * P, :], in_=ostg[sl]),
                  reads=(b_ostg[sl],), writes=())

    emit_xbar_b(0)
    for b in range(NT_B):
        phase_b_tile(b)

    fin = [(("q", "ost0"), k.qcnt.get("ost0", 0)), (("q", "ost1"), k.qcnt.get("ost1", 0))]
    k.op("sync", lambda s: s.nop(), extra=fin)
    k.op("gpsimd", lambda g: g.nop(), extra=fin)

    k.emit(es)
    return nc, es


def kernel(L0_attn_x, L0_attn_Wq, L0_attn_k_weight, L0_attn_v_weight, L0_attn_Wg,
           L0_attn_Wout, L0_ffn_gate_W_packed, L0_ffn_gate_scales, L0_ffn_up_W_packed,
           L0_ffn_up_scales, L0_ffn_down_W_packed, L0_ffn_down_scales):
    x = np.ascontiguousarray(np.asarray(L0_attn_x, dtype=np.float32))
    weights = dict(
        wq=np.ascontiguousarray(np.asarray(L0_attn_Wq, dtype=np.float32)),
        wk=np.ascontiguousarray(np.asarray(L0_attn_k_weight, dtype=np.float32)),
        wv=np.ascontiguousarray(np.asarray(L0_attn_v_weight, dtype=np.float32)),
        wg=np.ascontiguousarray(np.asarray(L0_attn_Wg, dtype=np.float32)),
        wo=np.ascontiguousarray(np.asarray(L0_attn_Wout, dtype=np.float32)),
        gp=np.ascontiguousarray(np.asarray(L0_ffn_gate_W_packed, dtype=np.int32)),
        gs=np.ascontiguousarray(np.asarray(L0_ffn_gate_scales, dtype=np.float32)),
        up=np.ascontiguousarray(np.asarray(L0_ffn_up_W_packed, dtype=np.int32)),
        us=np.ascontiguousarray(np.asarray(L0_ffn_up_scales, dtype=np.float32)),
        dp=np.ascontiguousarray(np.asarray(L0_ffn_down_W_packed, dtype=np.int32)),
        ds=np.ascontiguousarray(np.asarray(L0_ffn_down_scales, dtype=np.float32)),
    )
    nc, es = build_nc(debug=False)
    try:
        in_maps = [dict(weights, x=np.ascontiguousarray(x[b])) for b in range(8)]
        res = run_bass_kernel_spmd(nc, in_maps, core_ids=list(range(8)))
    finally:
        es.close()
    return np.stack([r["out"] for r in res.results], axis=0)


if __name__ == "__main__":
    rng = np.random.RandomState(0)
    print("smoke build only")
    nc, es = build_nc()
    es.close()
    print("built ok")
